# version 13
# baseline (speedup 1.0000x reference)
import numpy as np
import contextlib
import concourse.bass as bass
import concourse.mybir as mybir
from concourse.bass_utils import run_bass_kernel_spmd

F32 = mybir.dt.float32
BF16 = mybir.dt.bfloat16
AF = mybir.ActivationFunctionType
ALU = mybir.AluOpType
AX = mybir.AxisListType

import os
NH_EFF = int(os.environ.get("K_NH", "8"))
D = 1024
N = 4096
T = 2048
DHID = 4096
NH = 8
SCALE = 0.125
EPS_LN = 1e-5
EPS_ATT = 1e-8

A_K = 0
A_V = 16384
LNT = 33024
WKV = 41216
WQS = 49408
QST = 53504
KAUG = 33024
QSB = 41216
PT = 49408
AOST = 65792
W3ST = 65920
PSTG = 67968
WOS2 = 70016
AOT = 33024
LN2ST = 53504
A_G = 33024
A2F = 65792
TS1 = 82176
TS2 = 83200
BF_COLS = 84480

Z_STAGE = 0
Z_SQ = 2048
Z_BOBC = 3072
Z_EXTRA = 4096
Z_STATS = 5120
Z_IDENT = 5216
Z_SVEC = 5344
Z_GM = 5480
Z_SMALL = 5512
Z_X1B = 5536
FP_COLS = 7584

SEM_NAMES = ("pe", "ve", "sc", "po", "cc",
             "svec", "sbo", "sx0", "sx1", "ln1w", "lnt0", "lnt1",
             "gin", "gpan", "qw0", "qw1", "qsb", "qsb1", "ao0", "ao1",
             "w3l", "w3w", "yw0", "yw1", "yr", "x1w", "x1w1", "ln2w", "ln1w1", "ln2w1",
             "ff0", "ff1", "xr0", "xr1", "sout",
             "pst0", "pst1", "pw0", "pw1", "gw1", "gw2", "gh1", "gh2")


class Ctr:
    __slots__ = ("h", "v", "name")
    def __init__(self, h, name):
        self.h, self.v, self.name = h, 0, name
    def bump(self, k):
        self.v += k
        return self.v


def build(phases=6):
    nc = bass.Bass("TRN2", target_bir_lowering=False)
    ap = lambda name, shape, dt, kind="ExternalInput": nc.dram_tensor(name, shape, dt, kind=kind).ap()

    x = ap("x", [N, D], F32)
    x_res = ap("x_res", [T, D], F32)
    wq = ap("wq", [D, 512], F32)
    wkv = ap("wkv", [D, D], F32)
    wo = ap("wo", [512, D], F32)
    bo = ap("bo", [D], F32)
    w1 = ap("w1", [DHID, D], F32)
    w2 = ap("w2", [DHID, D], F32)
    h1 = ap("h1", [DHID, DHID], F32)
    h2 = ap("h2", [DHID, DHID], F32)
    w3 = ap("w3", [D, DHID], F32)
    s1 = ap("s1", [DHID], F32)
    s2 = ap("s2", [DHID], F32)
    sh1 = ap("sh1", [DHID], F32)
    sh2 = ap("sh2", [DHID], F32)
    s3 = ap("s3", [D], F32)

    out = ap("out", [T, D], F32, "ExternalOutput")

    ln1_p = ap("ln1_p", [8, N, 128], BF16, "Internal")
    q_hbm = ap("q_hbm", [512, N], BF16, "Internal")
    ao_p = ap("ao_p", [4, N, 128], BF16, "Internal")
    y_int = ap("y_int", [N, D], F32, "Internal")
    y_rs = ap("y_rs", [T, D], F32, "Internal")
    x1_hbm = ap("x1_hbm", [T, D], F32, "Internal")
    ln2_p = ap("ln2_p", [8, T, 128], BF16, "Internal")
    w1p = ap("w1p", [8, DHID, 128], BF16, "Internal")
    w2p = ap("w2p", [8, DHID, 128], BF16, "Internal")
    h1p = ap("h1p", [32, DHID, 128], BF16, "Internal")
    h2p = ap("h2p", [32, DHID, 128], BF16, "Internal")
    w3p = ap("w3p", [32, D, 128], BF16, "Internal")
    w1T = ap("w1T", [16, 128, 2048], BF16, "Internal")
    w2T = ap("w2T", [16, 128, 2048], BF16, "Internal")
    h1T = ap("h1T", [16, 128, 8192], BF16, "Internal")
    h2T = ap("h2T", [16, 128, 8192], BF16, "Internal")
    w3T = ap("w3T", [4, 128, 8192], BF16, "Internal")

    taps = {}
    if phases == 2:
        taps["d_k"] = ap("d_k", [128, 16384], BF16, "ExternalOutput")
        taps["d_v"] = ap("d_v", [128, 16640], BF16, "ExternalOutput")
        taps["d_q"] = ap("d_q", [512, N], BF16, "ExternalOutput")
    elif phases == 3:
        taps["d_ao"] = ap("d_ao", [4, N, 128], BF16, "ExternalOutput")
    elif phases == 3.5:
        taps["d_y"] = ap("d_y", [T, D], F32, "ExternalOutput")
    elif phases == 4:
        taps["d_x1"] = ap("d_x1", [T, D], F32, "ExternalOutput")
        taps["d_ln2"] = ap("d_ln2", [8, T, 128], BF16, "ExternalOutput")
    elif phases == 5:
        taps["d_h"] = ap("d_h", [128, 32768], BF16, "ExternalOutput")
        taps["d_g1"] = ap("d_g1", [128, 32768], BF16, "ExternalOutput")

    EM = {k: [] for k in ("sync", "gpsimd", "tensor", "vector", "scalar")}
    es = contextlib.ExitStack()
    with es:
        B = es.enter_context(nc.sbuf_tensor("abf", [128, BF_COLS], BF16))
        Z = es.enter_context(nc.sbuf_tensor("azf", [128, FP_COLS], F32))
        PB = [es.enter_context(nc.psum_tensor(f"pb{i}", [128, 1024], F32)) for i in range(4)]
        SH = {n: es.enter_context(nc.semaphore(n)) for n in SEM_NAMES}
        S = {n: Ctr(SH[n], n) for n in SEM_NAMES}
        pe, ve, sc, po, cc = (S[n] for n in ("pe", "ve", "sc", "po", "cc"))

        def emit(eng, fn, inc=None, amt=1):
            if inc is None:
                EM[eng].append(fn)
                return None
            t = inc.bump(amt)
            EM[eng].append(lambda e, fn=fn, h=inc.h, a=amt: fn(e).then_inc(h, a))
            return t

        def wait(eng, ctr, v):
            if v:
                EM[eng].append(lambda e, h=ctr.h, v=v: e.wait_ge(h, v))

        def dma(eng, ctr, out_ap, in_ap):
            return emit(eng, lambda e, o=out_ap, i=in_ap: e.dma_start(out=o, in_=i), ctr, 16)

        def dmat(ctr, out_ap, in_ap):
            return emit("sync", lambda e, o=out_ap, i=in_ap: e.dma_start_transpose(o, i), ctr, 16)

        t_id0 = emit("gpsimd", lambda e: e.memset(Z[:, Z_IDENT:Z_IDENT + 128], 1.0), po)
        wait("gpsimd", po, t_id0)
        t_ident = emit("gpsimd", lambda e: e.affine_select(
            Z[:, Z_IDENT:Z_IDENT + 128], Z[:, Z_IDENT:Z_IDENT + 128],
            [[1, 128]], ALU.is_equal, 0.0, base=0, channel_multiplier=-1), po)
        t_ones1 = emit("gpsimd", lambda e: e.memset(Z[0:1, Z_EXTRA:Z_EXTRA + 128], 1.0), po)
        t_epsln = emit("gpsimd", lambda e: e.memset(Z[:, Z_SMALL + 16:Z_SMALL + 17], EPS_LN), po)

        t_prev_copy = 0
        t_prev_mm = 0
        for src, rows, off in ((s1, 32, 0), (s2, 32, 32), (sh1, 32, 64), (sh2, 32, 96), (s3, 8, 128)):
            wait("sync", pe, t_prev_mm)
            tld = dma("sync", S["svec"], Z[0:rows, Z_SQ:Z_SQ + 128],
                      src.rearrange("(c p) -> c p", p=128))
            wait("tensor", S["svec"], tld)
            wait("tensor", po, t_ident)
            wait("tensor", ve, t_prev_copy)
            t_prev_mm = emit("tensor", lambda e, r=rows: e.transpose(
                PB[2][:, 0:r], Z[0:r, Z_SQ:Z_SQ + 128], Z[0:r, Z_IDENT:Z_IDENT + r]), pe)
            wait("vector", pe, t_prev_mm)
            t_prev_copy = emit("vector", lambda e, r=rows, o=off: e.tensor_copy(
                Z[:, Z_SVEC + o:Z_SVEC + o + r], PB[2][:, 0:r]), ve)
        t_svec = t_prev_copy
        t_svec_pe = t_prev_mm

        t_bo_ld = dma("sync", S["sbo"], Z[0:1, Z_BOBC:Z_BOBC + 1024],
                      bo.rearrange("(o d) -> o d", o=1))
        wait("tensor", S["sbo"], t_bo_ld)
        wait("tensor", po, t_ones1)
        wait("tensor", ve, t_svec)
        t_bomm = 0
        for of in range(2):
            t_bomm = emit("tensor", lambda e, of=of: e.matmul(
                PB[2][:, of * 512:(of + 1) * 512], Z[0:1, Z_EXTRA:Z_EXTRA + 128],
                Z[0:1, Z_BOBC + of * 512:Z_BOBC + (of + 1) * 512], start=True, stop=True), pe)
        wait("vector", pe, t_bomm)
        t_bobc = emit("vector", lambda e: e.tensor_copy(Z[:, Z_BOBC:Z_BOBC + 1024], PB[2][:, 0:1024]), ve)

        for ic in range(8):
            dma("gpsimd", S["gin"], B[:, WKV + ic * 1024:WKV + (ic + 1) * 1024],
                wkv[ic * 128:(ic + 1) * 128, :])
        for ic in range(8):
            t_gin_all = dma("gpsimd", S["gin"], B[:, WQS + ic * 512:WQS + (ic + 1) * 512],
                            wq[ic * 128:(ic + 1) * 128, :])
        for c in range(4):
            t_gpan_all = dma("gpsimd", S["gpan"], B[:, WOS2 + c * 1024:WOS2 + (c + 1) * 1024],
                             wo[c * 128:(c + 1) * 128, :])

        def emit_ln(xs, dst_bf, par, sq_off):
            so_ = Z_STATS + par * 12
            ssum = Z[:, so_:so_ + 1]; ssq = Z[:, so_ + 1:so_ + 2]
            mean = Z[:, so_ + 2:so_ + 3]; m2 = Z[:, so_ + 3:so_ + 4]
            var = Z[:, so_ + 4:so_ + 5]; sd = Z[:, so_ + 5:so_ + 6]
            rstd = Z[:, so_ + 6:so_ + 7]; negmr = Z[:, so_ + 7:so_ + 8]
            sq = Z[:, sq_off:sq_off + 1024]
            emit("vector", lambda e: e.reduce_sum(ssum, xs, axis=AX.X))
            t_sqw = emit("scalar", lambda e: e.activation(sq, xs, AF.Square, bias=0.0, scale=1.0), sc)
            wait("vector", sc, t_sqw)
            t_ssq = emit("vector", lambda e: e.reduce_sum(ssq, sq, axis=AX.X), ve)
            wait("vector", ve, t_ssq)
            emit("vector", lambda e: e.tensor_single_scalar(mean, ssum, 1.0 / D, ALU.mult))
            t2 = emit("vector", lambda e: e.tensor_single_scalar(var, ssq, 1.0 / D, ALU.mult), ve)
            wait("vector", ve, t2)
            t_m = emit("vector", lambda e: e.tensor_tensor(m2, mean, mean, ALU.mult), ve)
            wait("vector", ve, t_m)
            t_v2 = emit("vector", lambda e: e.tensor_tensor(var, var, m2, ALU.subtract), ve)
            wait("scalar", ve, t_v2)
            t_sd = emit("scalar", lambda e: e.activation(
                sd, var, AF.Sqrt, bias=Z[:, Z_SMALL + 16:Z_SMALL + 17], scale=1.0), sc)
            wait("vector", sc, t_sd)
            t_r = emit("vector", lambda e: e.reciprocal(rstd, sd), ve)
            wait("vector", ve, t_r)
            t_n = emit("vector", lambda e: e.tensor_scalar(negmr, mean, rstd, -1.0, ALU.mult, ALU.mult), ve)
            wait("vector", ve, t_n)
            t_norm = emit("vector", lambda e: e.tensor_scalar(
                dst_bf, xs, rstd, negmr, ALU.mult, ALU.add), ve)
            return t_norm, t_ssq

        t_norm1 = [0] * 32
        t_pan1 = 0
        t_sq_user = 0
        for c in range(32):
            pbuf = c % 2
            sxc = S["sx0"] if pbuf == 0 else S["sx1"]
            xs = Z[:, Z_STAGE + pbuf * 1024:Z_STAGE + (pbuf + 1) * 1024]
            lns = B[:, 61696 + pbuf * 1024:61696 + (pbuf + 1) * 1024]
            if c >= 2:
                wait("sync", ve, t_norm1[c - 2])
            t_ld = dma("sync", sxc, xs, x[c * 128:(c + 1) * 128, :])
            wait("vector", sxc, t_ld)
            wait("scalar", sxc, t_ld)
            if c == 0:
                wait("scalar", po, t_epsln)
                wait("scalar", pe, t_svec_pe)
            if c >= 1:
                wait("scalar", ve, t_sq_user)
            spn = S["ln1w"] if pbuf == 0 else S["ln1w1"]
            if c >= 2:
                wait("vector", spn, 16 * 8 * (c // 2))
            t_norm1[c], t_sq_user = emit_ln(xs, lns, pbuf, Z_SQ)
            wait("sync", ve, t_norm1[c])
            if c >= 2:
                wait("sync", spn, 16 * 8 * (c // 2))
            for cc_ in range(8):
                t_pan1 = dma("sync", spn, ln1_p[cc_, c * 128:(c + 1) * 128, :],
                             lns[:, cc_ * 128:(cc_ + 1) * 128])
        t_ln1_totals = (S["ln1w"].v, S["ln1w1"].v)

        KT = lambda it: B[:, A_K + it * 4096:A_K + (it + 1) * 4096]
        def vaug(kc):
            return B[:, A_V + kc * 520:A_V + (kc + 1) * 520].rearrange("p (h d) -> p h d", d=65)
        t_vones = 0
        for kc in range(32):
            t_vones = emit("gpsimd", lambda e, kc=kc: e.memset(vaug(kc)[:, :, 64:65], 1.0), po)

        t_lnT_used = [0, 0]
        pk_free = [0, 0, 0, 0]
        t_kevac_all = 0
        t_vevac_all = 0
        t_qw = [0, 0]
        t_qevac = [0, 0]
        t_p2_mm_last = 0
        t_lnld = [0, 0]
        for s in range(8):
            lb = s % 2
            slt = S["lnt0"] if lb == 0 else S["lnt1"]
            lnT = B[:, LNT + lb * 4096:LNT + (lb + 1) * 4096]
            wait("sync", pe, t_lnT_used[lb])
            wait("sync", S["ln1w"], min(16 * 8 * (2 * s + 2), t_ln1_totals[0]))
            wait("sync", S["ln1w1"], min(16 * 8 * (2 * s + 2), t_ln1_totals[1]))
            for ic in range(8):
                t_lnld[lb] = dmat(slt, lnT[:, ic * 512:(ic + 1) * 512],
                                  ln1_p[ic, s * 512:(s + 1) * 512, :])
            for it in range(4):
                pb_i = it % 2
                wait("tensor", slt, t_lnld[lb])
                wait("tensor", S["gin"], t_gin_all)
                wait("tensor", sc, pk_free[pb_i])
                for ic in range(8):
                    tm = emit("tensor", lambda e, pb_i=pb_i, ic=ic, it=it, lnT=lnT: e.matmul(
                        PB[pb_i][:, 0:512],
                        B[:, WKV + ic * 1024 + it * 128:WKV + ic * 1024 + (it + 1) * 128],
                        lnT[:, ic * 512:(ic + 1) * 512],
                        start=(ic == 0), stop=(ic == 7)), pe)
                wait("scalar", pe, tm)
                t_kevac_all = emit("scalar", lambda e, it=it, s=s, pb_i=pb_i: e.activation(
                    KT(it)[:, s * 512:(s + 1) * 512], PB[pb_i][:, 0:512],
                    AF.Copy, bias=0.0, scale=1.0), sc)
                pk_free[pb_i] = t_kevac_all
            for ks in range(4):
                kc = s * 4 + ks
                pb_i = 2 + ks % 2
                wait("tensor", sc, pk_free[pb_i])
                for ic in range(8):
                    tm = emit("tensor", lambda e, pb_i=pb_i, ic=ic, ks=ks, lnT=lnT: e.matmul(
                        PB[pb_i][:, 0:512],
                        lnT[:, ic * 512 + ks * 128:ic * 512 + (ks + 1) * 128],
                        B[:, WKV + ic * 1024 + 512:WKV + (ic + 1) * 1024],
                        start=(ic == 0), stop=(ic == 7)), pe)
                wait("scalar", pe, tm)
                wait("scalar", po, t_vones)
                t_vevac_all = emit("scalar", lambda e, kc=kc, pb_i=pb_i: e.activation(
                    vaug(kc)[:, 0:8, 0:64], PB[pb_i][:, 0:512],
                    AF.Copy, bias=0.0, scale=1.0), sc)
                pk_free[pb_i] = t_vevac_all
            for it in range(4):
                pb_i = it % 2
                qb = it % 2
                sqw = S["qw0"] if qb == 0 else S["qw1"]
                wait("tensor", sc, pk_free[pb_i])
                wait("tensor", ve, t_qevac[qb])
                for ic in range(8):
                    tm = emit("tensor", lambda e, pb_i=pb_i, ic=ic, it=it, lnT=lnT: e.matmul(
                        PB[pb_i][:, 0:512],
                        B[:, WQS + ic * 512 + it * 128:WQS + ic * 512 + (it + 1) * 128],
                        lnT[:, ic * 512:(ic + 1) * 512],
                        start=(ic == 0), stop=(ic == 7)), pe)
                t_p2_mm_last = tm
                wait("scalar", pe, tm)
                wait("scalar", sqw, t_qw[qb])
                t_qevac[qb] = emit("scalar", lambda e, qb=qb, pb_i=pb_i: e.activation(
                    B[:, QST + qb * 512:QST + (qb + 1) * 512], PB[pb_i][:, 0:512],
                    AF.Copy, bias=0.0, scale=SCALE), sc)
                pk_free[pb_i] = t_qevac[qb]
                wait("sync", sc, t_qevac[qb])
                t_qw[qb] = dma("sync", sqw, q_hbm[it * 128:(it + 1) * 128, s * 512:(s + 1) * 512],
                               B[:, QST + qb * 512:QST + (qb + 1) * 512])
            t_lnT_used[lb] = t_p2_mm_last

        if phases == 2:
            wait("sync", sc, t_kevac_all)
            wait("sync", sc, t_vevac_all)
            dma("sync", S["sout"], taps["d_k"], B[:, A_K:A_K + 16384])
            dma("sync", S["sout"], taps["d_v"], B[:, A_V:A_V + 16640])
            wait("sync", S["qw0"], t_qw[0])
            wait("sync", S["qw1"], t_qw[1])
            t3 = dma("sync", S["sout"], taps["d_q"], q_hbm)
            wait("sync", S["sout"], t3)
            _finalize(nc, es, EM)
            return nc, taps

        wait("gpsimd", pe, t_p2_mm_last)
        t_kaug_ones = emit("gpsimd", lambda e: e.memset(B[64:65, KAUG:KAUG + 8192], 1.0), po)
        cast_jobs = ([("w1", w1p, w1, c) for c in range(8)] +
                     [("w2", w2p, w2, c) for c in range(8)] +
                     [("h1", h1p, h1, c) for c in range(32)] +
                     [("h2", h2p, h2, c) for c in range(32)])
        cast_state = {"idx": 0}
        cast_sem = {"w1": "gw1", "w2": "gw2", "h1": "gh1", "h2": "gh2"}

        def emit_cast(n):
            for _ in range(n):
                k = cast_state["idx"]
                if k >= len(cast_jobs):
                    return
                cast_state["idx"] = k + 1
                nm, wp_, wsrc, c = cast_jobs[k]
                dma("gpsimd", S[cast_sem[nm]], wp_[c], wsrc[:, c * 128:(c + 1) * 128])
                if k + 1 >= len(cast_jobs) or cast_jobs[k + 1][0] != nm:
                    cast_totals[nm] = S[cast_sem[nm]].v
        pv_ps = [PB[3][:, 0:65], PB[3][:, 512:577]]
        p2s = lambda r: PB[2][:, r * 512:(r + 1) * 512]

        pT_lastread = [0] * 32
        t_kaug = [0, 0]
        t_pass2_last = [0] * NH
        t_gmrow = [0] * 32
        t_comb = [0] * 32
        p1_free = [0, 0]
        p2_free = [0, 0, 0, 0]
        t_exp = {}
        t_mask = {}
        t_pv_free = [0, 0]
        t_pv_l = [0, 0]
        t_ao_w = [0, 0]
        t_qload = [0, 0]
        gcount = 0

        w3_jobs = [(blk, hf2) for blk in range(8) for hf2 in range(2)]
        w3_state = {"stage_free": 0, "last_w": 0}
        t_w3_total = 0

        def emit_w3(jix):
            nonlocal t_w3_total
            blk, hf2 = w3_jobs[jix]
            zst = Z[:, Z_STAGE:Z_STAGE + 2048]
            if jix == 0:
                wait("sync", ve, t_norm1[31])
                wait("scalar", S["qw0"], t_qw[0])
                wait("scalar", S["qw1"], t_qw[1])
            wait("sync", sc, w3_state["stage_free"])
            tld = dma("sync", S["w3l"], zst, w3[blk * 128:(blk + 1) * 128, hf2 * 2048:(hf2 + 1) * 2048])
            wait("scalar", S["w3l"], tld)
            wait("scalar", ve, t_svec)
            wait("scalar", S["w3w"], w3_state["last_w"])
            tc = emit("scalar", lambda e, blk=blk: e.activation(
                B[:, W3ST:W3ST + 2048], zst, AF.Copy, bias=0.0,
                scale=Z[:, Z_SVEC + 128 + blk:Z_SVEC + 128 + blk + 1]), sc)
            w3_state["stage_free"] = tc
            wait("sync", sc, tc)
            tw = 0
            for cc_ in range(16):
                hc = hf2 * 16 + cc_
                tw = dma("sync", S["w3w"], w3p[hc, blk * 128:(blk + 1) * 128, :],
                         B[:, W3ST + cc_ * 128:W3ST + (cc_ + 1) * 128])
            w3_state["last_w"] = tw
            t_w3_total = tw

        P3PART = int(os.environ.get("K_P3PART", "30"))
        prep_jobs = []
        for nm_, wp_, wT_, npan in (("w1", w1p, w1T, 8), ("w2", w2p, w2T, 8),
                                    ("h1", h1p, h1T, 32), ("h2", h2p, h2T, 32)):
            for p_ in range(npan):
                for r_ in range(4):
                    prep_jobs.append((nm_, wp_, wT_, p_, r_, npan))
        prep_state = {"idx": 0, "wr": [(None, 0), (None, 0)], "gated": set()}
        cast_totals = {}

        def emit_prep(n):
            for _ in range(n):
                k = prep_state["idx"]
                if k >= len(prep_jobs):
                    return
                prep_state["idx"] = k + 1
                nm_, wp_, wT_, p_, r_, npan = prep_jobs[k]
                bufp = k % 2
                spst = S["pst0"] if bufp == 0 else S["pst1"]
                spw = S["pw0"] if bufp == 0 else S["pw1"]
                stg = B[:, PSTG + bufp * 1024:PSTG + (bufp + 1) * 1024]
                if nm_ not in prep_state["gated"]:
                    while nm_ not in cast_totals:
                        emit_cast(1)
                    wait("sync", S[cast_sem[nm_]], cast_totals[nm_])
                    prep_state["gated"].add(nm_)
                ws, wt = prep_state["wr"][bufp]
                if ws is not None:
                    wait("sync", ws, wt)
                tld = dmat(spst, stg, wp_[p_, r_ * 1024:(r_ + 1) * 1024, :])
                wait("sync", spst, tld)
                tw = 0
                for q_ in range(4):
                    tw = dma("sync", spw, wT_[r_ * 4 + q_][:, p_ * 256:(p_ + 1) * 256],
                             stg[:, q_ * 256:(q_ + 1) * 256])
                prep_state["wr"][bufp] = (spw, tw)

        prev_comb = [0] * 32

        def gen_pass1(i):
            nonlocal gcount
            hcur = i
            kb = hcur % 2
            qsb_o = QSB + (hcur % 2) * 4096
            for qq in range(32):
                par = qq % 2
                pcols = Z_SMALL + par * 4
                if qq >= 2:
                    wait("vector", ve, t_comb[qq - 2])
                elif i > 0:
                    wait("vector", ve, prev_comb[30 + qq])
                t_r = 0
                for g in range(4):
                    pb_i = gcount % 2
                    gcount += 1
                    wait("tensor", S["qsb"] if hcur % 2 == 0 else S["qsb1"], t_qload[hcur % 2])
                    wait("tensor", ve, t_kaug[kb])
                    wait("tensor", po, t_kaug_ones)
                    wait("tensor", ve, p1_free[pb_i])
                    for sub in range(2):
                        tmm = emit("tensor", lambda e, pb_i=pb_i, qq=qq, g=g, sub=sub, kb=kb, qsb_o=qsb_o: e.matmul(
                            PB[pb_i][:, sub * 512:(sub + 1) * 512],
                            B[0:64, qsb_o + qq * 128:qsb_o + (qq + 1) * 128],
                            B[0:64, KAUG + kb * 4096 + (g * 2 + sub) * 512:KAUG + kb * 4096 + (g * 2 + sub + 1) * 512],
                            start=True, stop=True), pe)
                    wait("vector", pe, tmm)
                    t_r = emit("vector", lambda e, pcols=pcols, g=g, pb_i=pb_i: e.reduce_max(
                        Z[:, pcols + g:pcols + g + 1], PB[pb_i][:, 0:1024], axis=AX.X), ve)
                    p1_free[pb_i] = t_r
                    yield
                wait("vector", ve, t_r)
                t_comb[qq] = emit("vector", lambda e, pcols=pcols, qq=qq: e.reduce_max(
                    Z[:, Z_GM + qq:Z_GM + qq + 1], Z[:, pcols:pcols + 4],
                    axis=AX.X, negate=True), ve)

        def gen_work(h):
            qsb_h = QSB + (h % 2) * 4096
            wait("tensor", ve, prev_comb[31])
            wait("tensor", sc, p2_free[0])
            for qq in range(32):
                if qq > 0:
                    wait("tensor", ve, t_gmrow[qq - 1])
                tmm = emit("tensor", lambda e, qq=qq: e.transpose(
                    PB[2][0:1, 0:128], Z[:, Z_GM + qq:Z_GM + qq + 1],
                    Z[:, Z_IDENT:Z_IDENT + 128]), pe)
                wait("vector", pe, tmm)
                t_gmrow[qq] = emit("vector", lambda e, qq=qq, qsb_h=qsb_h: e.tensor_copy(
                    B[64:65, qsb_h + qq * 128:qsb_h + (qq + 1) * 128], PB[2][0:1, 0:128]), ve)
                if qq % 2 == 1:
                    yield
            for j in range(8):
                g0 = j * 512
                kcs = list(range(4 * j + 4))
                for gbase in range(0, len(kcs), 2):
                    grp = kcs[gbase:gbase + 2]
                    wait("tensor", sc, p2_free[0])
                    wait("tensor", ve, t_gmrow[31])
                    wait("tensor", ve, t_kaug[h % 2])
                    wait("tensor", po, t_kaug_ones)
                    for gi, kc in enumerate(grp):
                        tmm = emit("tensor", lambda e, kc=kc, j=j, r=gi, kb=h % 2, qsb_h=qsb_h: e.matmul(
                            p2s(r),
                            B[0:65, KAUG + kb * 4096 + kc * 128:KAUG + kb * 4096 + (kc + 1) * 128],
                            B[0:65, qsb_h + j * 512:qsb_h + (j + 1) * 512],
                            start=True, stop=True), pe if gi == len(grp) - 1 else None)
                    t_pass2_last[h] = tmm
                    wait("scalar", pe, tmm)
                    wait("scalar", pe, max(pT_lastread[kc] for kc in grp))
                    for gi, kc in enumerate(grp):
                        texp = emit("scalar", lambda e, kc=kc, r=gi: e.activation(
                            B[:, PT + kc * 512:PT + (kc + 1) * 512], p2s(r),
                            AF.Exp, bias=0.0, scale=1.0), sc if gi == len(grp) - 1 else None)
                    for kc in grp:
                        t_exp[kc] = texp
                    p2_free[0] = texp
                    for kc in grp:
                        if kc >= 4 * j:
                            wait("gpsimd", sc, texp)
                            t_mask[kc] = emit("gpsimd", lambda e, kc=kc, bs=g0 - kc * 128: e.affine_select(
                                B[:, PT + kc * 512:PT + (kc + 1) * 512],
                                B[:, PT + kc * 512:PT + (kc + 1) * 512],
                                [[1, 512]], ALU.is_ge, 0.0, base=bs, channel_multiplier=-1), po)
                        else:
                            t_mask[kc] = None
                    yield
                for rr_ in range(4):
                    qq = 4 * j + rr_
                    kl = list(range(qq + 1))
                    pvb = qq % 2
                    pv = pv_ps[pvb]
                    wait("tensor", sc, t_pv_free[pvb])
                    wait("tensor", ve, t_pv_l[pvb])
                    nmm = 0
                    for idx, kc in enumerate(kl):
                        wait("tensor", sc, t_exp[kc])
                        if t_mask[kc] is not None:
                            wait("tensor", po, t_mask[kc])
                        tmm = emit("tensor", lambda e, pv=pv, kc=kc, rr_=rr_, hh2=h,
                                   st=(idx == 0), sp=(idx == len(kl) - 1): e.matmul(
                            pv,
                            B[:, PT + kc * 512 + rr_ * 128:PT + kc * 512 + rr_ * 128 + 128],
                            B[:, A_V + kc * 520 + hh2 * 65:A_V + kc * 520 + (hh2 + 1) * 65],
                            start=st, stop=sp), pe)
                        pT_lastread[kc] = tmm
                        nmm += 1
                        if nmm % 4 == 0:
                            yield
                    le = Z[:, Z_SMALL + 8 + pvb:Z_SMALL + 9 + pvb]
                    li = Z[:, Z_SMALL + 12 + pvb:Z_SMALL + 13 + pvb]
                    wait("vector", pe, tmm)
                    wait("vector", sc, t_pv_free[pvb])
                    tv1 = emit("vector", lambda e, le=le, pv=pv: e.tensor_single_scalar(
                        le, pv[:, 64:65], EPS_ATT, ALU.add), ve)
                    wait("vector", ve, tv1)
                    tv2 = emit("vector", lambda e, li=li, le=le: e.reciprocal(li, le), ve)
                    t_pv_l[pvb] = tv2
                    aob = qq % 2
                    sao = S["ao0"] if aob == 0 else S["ao1"]
                    aos = B[:, AOST + aob * 64:AOST + (aob + 1) * 64]
                    wait("scalar", ve, tv2)
                    wait("scalar", sao, t_ao_w[aob])
                    tse = emit("scalar", lambda e, aos=aos, pv=pv, li=li: e.activation(
                        aos, pv[:, 0:64], AF.Copy, bias=0.0, scale=li), sc)
                    t_pv_free[pvb] = tse
                    wait("sync", sc, tse)
                    t_ao_w[aob] = dma("sync", sao,
                                      ao_p[h // 2, qq * 128:(qq + 1) * 128,
                                           (h % 2) * 64:(h % 2) * 64 + 64], aos)
                    yield

        for i in range(NH_EFF + 1):
            h = i - 1
            prev_comb = list(t_comb)
            if i < NH_EFF:
                hcur = i
                kb = hcur % 2
                it = hcur // 2
                rbase = (hcur % 2) * 64
                qsb_o = QSB + (hcur % 2) * 4096
                wait("vector", sc, t_kevac_all)
                if i >= 2:
                    wait("vector", pe, t_pass2_last[i - 2])
                else:
                    wait("vector", pe, t_p2_mm_last)
                t_kaug[kb] = emit("vector", lambda e, kb=kb, it=it, rbase=rbase: e.tensor_copy(
                    B[0:64, KAUG + kb * 4096:KAUG + (kb + 1) * 4096],
                    B[rbase:rbase + 64, A_K + it * 4096:A_K + (it + 1) * 4096]), ve)
                if i == 0:
                    wait("sync", pe, t_p2_mm_last)
                    wait("sync", S["qw0"], t_qw[0])
                    wait("sync", S["qw1"], t_qw[1])
                    t_qload[0] = dma("sync", S["qsb"], B[0:64, qsb_o:qsb_o + 4096],
                                     q_hbm[hcur * 64:(hcur + 1) * 64, :])
            ga = gen_pass1(i) if i < NH_EFF else None
            gb = gen_work(h) if h >= 0 else None
            na = nb = True
            ticks = 0
            while (ga is not None and na) or (gb is not None and nb):
                if ga is not None and na:
                    try:
                        next(ga)
                    except StopIteration:
                        na = False
                nburst = 1 if na else 3
                for _ in range(nburst):
                    if gb is not None and nb:
                        try:
                            next(gb)
                        except StopIteration:
                            nb = False
                ticks += 1
                if ticks % 12 == 0:
                    emit_cast(1)
                if i >= 3 and ticks % 2 == 0:
                    emit_prep(1)
            if h >= 0:
                emit_w3(2 * h)
                emit_w3(2 * h + 1)
            if i + 1 < NH_EFF:
                nx = i + 1
                qsb_n = QSB + (nx % 2) * 4096
                wait("sync", pe, t_pass2_last[nx - 2] if nx >= 2 else t_p2_mm_last)
                wait("sync", S["qw0"], t_qw[0])
                wait("sync", S["qw1"], t_qw[1])
                sqb_n = S["qsb"] if nx % 2 == 0 else S["qsb1"]
                t_qload[nx % 2] = dma("sync", sqb_n, B[0:64, qsb_n:qsb_n + 4096],
                                      q_hbm[nx * 64:(nx + 1) * 64, :])

        for jix in range(2 * NH, len(w3_jobs)):
            emit_w3(jix)
        emit_cast(len(cast_jobs))
        emit_prep(len(prep_jobs))

        if phases == 3:
            wait("sync", S["ao0"], t_ao_w[0])
            wait("sync", S["ao1"], t_ao_w[1])
            tl = dma("sync", S["sout"], taps["d_ao"], ao_p)
            wait("sync", S["sout"], tl)
            _finalize(nc, es, EM)
            return nc, taps

        t_aot = 0
        wait("sync", S["ao0"], t_ao_w[0])
        wait("sync", S["ao1"], t_ao_w[1])
        for c in range(4):
            t_aot = dmat(S["qsb"], B[:, AOT + c * 4096:AOT + (c + 1) * 4096], ao_p[c])
        y_free = [0, 0]
        t_yw = [0, 0]
        for tt in range(32):
            pb_i = tt % 2
            syw = S["yw0"] if pb_i == 0 else S["yw1"]
            yst = Z[:, Z_STAGE + pb_i * 1024:Z_STAGE + (pb_i + 1) * 1024]
            wait("tensor", S["qsb"], t_aot)
            wait("tensor", S["gpan"], t_gpan_all)
            wait("tensor", ve, y_free[pb_i])
            for of in range(2):
                for c in range(4):
                    tmm = emit("tensor", lambda e, pb_i=pb_i, of=of, c=c, tt=tt: e.matmul(
                        PB[pb_i][:, of * 512:(of + 1) * 512],
                        B[:, AOT + c * 4096 + tt * 128:AOT + c * 4096 + (tt + 1) * 128],
                        B[:, WOS2 + c * 1024 + of * 512:WOS2 + c * 1024 + (of + 1) * 512],
                        start=(c == 0), stop=(c == 3)), pe)
            wait("vector", pe, tmm)
            wait("vector", syw, t_yw[pb_i])
            t_ycp = emit("vector", lambda e, yst=yst, pb_i=pb_i: e.tensor_copy(yst, PB[pb_i][:, 0:1024]), ve)
            y_free[pb_i] = t_ycp
            wait("sync", ve, t_ycp)
            t_yw[pb_i] = dma("sync", syw, y_int[tt * 128:(tt + 1) * 128, :], yst)

        for hc in range(32):
            bufp = hc % 2
            spst = S["pst0"] if bufp == 0 else S["pst1"]
            spw = S["pw0"] if bufp == 0 else S["pw1"]
            stg = B[:, PSTG + bufp * 1024:PSTG + (bufp + 1) * 1024]
            if hc == 0:
                wait("sync", S["w3w"], t_w3_total)
            ws, wt = prep_state["wr"][bufp]
            if ws is not None:
                wait("sync", ws, wt)
            tld = dmat(spst, stg, w3p[hc])
            wait("sync", spst, tld)
            tw = dma("sync", spw, w3T[hc // 8][:, (hc % 8) * 1024:(hc % 8 + 1) * 1024], stg)
            prep_state["wr"][bufp] = (spw, tw)
        t_pw_totals = (S["pw0"].v, S["pw1"].v)

        wait("gpsimd", S["yw0"], t_yw[0])
        wait("gpsimd", S["yw1"], t_yw[1])
        t_cc = emit("gpsimd", lambda e: e.collective_compute(
            "ReduceScatter", ALU.add,
            replica_groups=[[0, 1], [2, 3], [4, 5], [6, 7]],
            ins=[y_int.opt()], outs=[y_rs.opt()]), cc, 1)

        if phases == 3.5:
            wait("sync", cc, t_cc)
            tl = dma("sync", S["sout"], taps["d_y"], y_rs)
            wait("sync", S["sout"], tl)
            _finalize(nc, es, EM)
            return nc, taps

        t_norm2 = [0] * 16
        t_pan2 = 0
        t_x1w = [0] * 16
        t_add2 = 0
        t_sq_user2 = 0
        for t_ in range(16):
            ych = Z[:, Z_STAGE:Z_STAGE + 1024]
            xrch = Z[:, Z_STAGE + 1024:Z_STAGE + 2048]
            x1ch = Z[:, Z_X1B + (t_ % 2) * 1024:Z_X1B + (t_ % 2 + 1) * 1024]
            if t_ >= 1:
                wait("sync", ve, t_add2)
            if t_ >= 2:
                wait("sync", ve, t_norm2[t_ - 2])
                wait("sync", S["x1w"] if t_ % 2 == 0 else S["x1w1"], t_x1w[t_ - 2])
            wait("sync", cc, t_cc)
            dma("sync", S["yr"], ych, y_rs[t_ * 128:(t_ + 1) * 128, :])
            t_l2 = dma("sync", S["yr"], xrch, x_res[t_ * 128:(t_ + 1) * 128, :])
            wait("vector", S["yr"], t_l2)
            wait("vector", ve, t_bobc)
            if t_ >= 2:
                wait("vector", ve, t_norm2[t_ - 2])
            if t_ >= 1:
                wait("vector", ve, t_sq_user2)
            t_a1 = emit("vector", lambda e, ych=ych, xrch=xrch, x1ch=x1ch: e.tensor_tensor(
                x1ch, ych, xrch, ALU.add), ve)
            wait("vector", ve, t_a1)
            t_add2 = emit("vector", lambda e, x1ch=x1ch: e.tensor_tensor(
                x1ch, x1ch, Z[:, Z_BOBC:Z_BOBC + 1024], ALU.add), ve)
            wait("vector", ve, t_add2)
            wait("scalar", ve, t_add2)
            ln2s = B[:, LN2ST:LN2ST + 1024]
            spn2 = S["ln2w"] if t_ % 2 == 0 else S["ln2w1"]
            oth2 = S["ln2w1"] if t_ % 2 == 0 else S["ln2w"]
            if t_ >= 1:
                wait("vector", oth2, 16 * 8 * ((t_ + 1) // 2))
            if t_ >= 2:
                wait("vector", spn2, 16 * 8 * (t_ // 2))
            t_norm2[t_], t_sq_user2 = emit_ln(x1ch, ln2s, t_ % 2, Z_EXTRA)
            wait("sync", ve, t_norm2[t_])
            t_x1w[t_] = dma("sync", S["x1w"] if t_ % 2 == 0 else S["x1w1"],
                            x1_hbm[t_ * 128:(t_ + 1) * 128, :], x1ch)
            for cc_ in range(8):
                t_pan2 = dma("sync", spn2, ln2_p[cc_, t_ * 128:(t_ + 1) * 128, :],
                             ln2s[:, cc_ * 128:(cc_ + 1) * 128])
        t_ln2_totals = (S["ln2w"].v, S["ln2w1"].v)

        if phases == 4:
            wait("sync", S["x1w"], t_x1w[14])
            wait("sync", S["x1w1"], t_x1w[15])
            wait("sync", S["ln2w"], t_ln2_totals[0])
            wait("sync", S["ln2w1"], t_ln2_totals[1])
            dma("sync", S["sout"], taps["d_x1"], x1_hbm)
            tl = dma("sync", S["sout"], taps["d_ln2"], ln2_p)
            wait("sync", S["sout"], tl)
            _finalize(nc, es, EM)
            return nc, taps

        HT = lambda c: B[:, A_K + c * 1024:A_K + (c + 1) * 1024]
        GT = lambda c: B[:, A_G + c * 1024:A_G + (c + 1) * 1024]
        t_out_last = 0
        t_w3_used = 0
        t_h2_mm_last = 0
        t_ow = 0
        t_oadd = 0
        for hf in range(2):
            tb = hf * 1024
            wait("sync", S["ln2w"], t_ln2_totals[0])
            wait("sync", S["ln2w1"], t_ln2_totals[1])
            if hf == 1:
                wait("sync", pe, t_h2_mm_last)
                wait("sync", pe, t_w3_used)
            if hf == 0:
                wait("sync", S["pw0"], t_pw_totals[0])
                wait("sync", S["pw1"], t_pw_totals[1])
                wait("sync", S["pst0"], S["pst0"].v)
                wait("sync", S["pst1"], S["pst1"].v)
            t_lnT2 = 0
            for c in range(8):
                t_lnT2 = dmat(S["lnt0"], B[:, A2F + c * 1024:A2F + (c + 1) * 1024],
                              ln2_p[c, tb:tb + 1024, :])
            t_w1t = [0, 0]
            t_wstream_used = [0, 0]
            f2_free = [0, 0]
            t_h_evt = 0
            t_ts_free = 0
            for hb in range(16):
                wb = hb % 2
                sff = S["ff0"] if wb == 0 else S["ff1"]
                W1T = A2F + 8192 + wb * 2048
                W2T = A2F + 12288 + wb * 2048
                wait("sync", pe, t_wstream_used[wb])
                if hf == 0 and hb < 2:
                    wait("sync", S["pw0"], t_pw_totals[0])
                    wait("sync", S["pw1"], t_pw_totals[1])
                dma("sync", sff, B[:, W1T:W1T + 2048], w1T[hb])
                t_w1t[wb] = dma("sync", sff, B[:, W2T:W2T + 2048], w2T[hb])
                for sub in range(2):
                    hrow = hb * 2 + sub
                    for tg in range(2):
                        pb_i = (hrow * 2 + tg) % 2
                        wait("tensor", sff, t_w1t[wb])
                        wait("tensor", S["lnt0"], t_lnT2)
                        wait("tensor", ve, f2_free[pb_i])
                        for c in range(8):
                            emit("tensor", lambda e, pb_i=pb_i, c=c, sub=sub, tg=tg, W1T=W1T: e.matmul(
                                PB[pb_i][:, 0:512],
                                B[:, W1T + c * 256 + sub * 128:W1T + c * 256 + (sub + 1) * 128],
                                B[:, A2F + c * 1024 + tg * 512:A2F + c * 1024 + (tg + 1) * 512],
                                start=(c == 0), stop=(c == 7)))
                        for c in range(8):
                            tmm = emit("tensor", lambda e, pb_i=pb_i, c=c, sub=sub, tg=tg, W2T=W2T: e.matmul(
                                PB[pb_i][:, 512:1024],
                                B[:, W2T + c * 256 + sub * 128:W2T + c * 256 + (sub + 1) * 128],
                                B[:, A2F + c * 1024 + tg * 512:A2F + c * 1024 + (tg + 1) * 512],
                                start=(c == 0), stop=(c == 7)), pe)
                        t_wstream_used[wb] = tmm
                        wait("scalar", pe, tmm)
                        wait("scalar", ve, t_ts_free)
                        tb_i = tg
                        sv = Z[:, Z_SVEC + hrow:Z_SVEC + hrow + 1]
                        sv2 = Z[:, Z_SVEC + 32 + hrow:Z_SVEC + 32 + hrow + 1]
                        if hf == 0 and hb == 0 and sub == 0 and tg == 0:
                            wait("scalar", ve, t_svec)
                        emit("scalar", lambda e, pb_i=pb_i, tb_i=tb_i, sv=sv: e.activation(
                            B[:, TS1 + tb_i * 512:TS1 + (tb_i + 1) * 512], PB[pb_i][:, 0:512],
                            AF.Silu, bias=0.0, scale=sv))
                        t_s2 = emit("scalar", lambda e, pb_i=pb_i, tb_i=tb_i, sv2=sv2: e.activation(
                            B[:, TS2 + tb_i * 512:TS2 + (tb_i + 1) * 512], PB[pb_i][:, 512:1024],
                            AF.Copy, bias=0.0, scale=sv2), sc)
                        wait("vector", sc, t_s2)
                        t_h_evt = emit("vector", lambda e, hrow=hrow, tg=tg, tb_i=tb_i: e.tensor_tensor(
                            HT(hrow)[:, tg * 512:(tg + 1) * 512],
                            B[:, TS1 + tb_i * 512:TS1 + (tb_i + 1) * 512],
                            B[:, TS2 + tb_i * 512:TS2 + (tb_i + 1) * 512], ALU.mult), ve)
                        f2_free[pb_i] = t_h_evt
                        t_ts_free = t_h_evt
            t_hdone = t_h_evt

            if phases == 5 and hf == 0:
                wait("sync", ve, t_hdone)
                dma("sync", S["sout"], taps["d_h"], B[:, A_K:A_K + 32768])

            t_prev_layer_mm = max(t_wstream_used)
            t_g1done = 0
            t_g2done = 0
            for layer in range(2):
                hp = h1p if layer == 0 else h2p
                svoff = 64 if layer == 0 else 96
                src_t = HT if layer == 0 else GT
                dst_t = GT if layer == 0 else HT
                t_ht = [0, 0]
                t_hstream_used = [t_prev_layer_mm, t_prev_layer_mm]
                f3_free = [0, 0]
                t_gevt = 0
                if layer == 0:
                    wait("tensor", ve, t_hdone)
                else:
                    wait("tensor", sc, t_g1done)
                hT_ = h1T if layer == 0 else h2T
                for ob in range(16):
                    wb = ob % 2
                    sff = S["ff0"] if wb == 0 else S["ff1"]
                    HTS = A2F + wb * 8192
                    wait("sync", pe, t_hstream_used[wb])
                    t_ht[wb] = dma("sync", sff, B[:, HTS:HTS + 8192], hT_[ob])
                    for sub in range(2):
                        orow = ob * 2 + sub
                        for tg in range(2):
                            pb_i = 2 + (orow * 2 + tg) % 2
                            wait("tensor", sff, t_ht[wb])
                            wait("tensor", sc, f3_free[pb_i - 2])
                            for c in range(32):
                                tmm = emit("tensor", lambda e, pb_i=pb_i, c=c, sub=sub, tg=tg,
                                           HTS=HTS, src_t=src_t: e.matmul(
                                    PB[pb_i][:, 0:512],
                                    B[:, HTS + c * 256 + sub * 128:HTS + c * 256 + (sub + 1) * 128],
                                    src_t(c)[:, tg * 512:(tg + 1) * 512],
                                    start=(c == 0), stop=(c == 31)), pe)
                            t_hstream_used[wb] = tmm
                            wait("scalar", pe, tmm)
                            sv = Z[:, Z_SVEC + svoff + orow:Z_SVEC + svoff + orow + 1]
                            t_gevt = emit("scalar", lambda e, pb_i=pb_i, orow=orow, tg=tg,
                                          sv=sv, dst_t=dst_t: e.activation(
                                dst_t(orow)[:, tg * 512:(tg + 1) * 512], PB[pb_i][:, 0:512],
                                AF.Gelu, bias=0.0, scale=sv), sc)
                            f3_free[pb_i - 2] = t_gevt
                t_prev_layer_mm = max(t_hstream_used)
                if layer == 0:
                    t_g1done = t_gevt
                else:
                    t_g2done = t_gevt
                    t_h2_mm_last = t_prev_layer_mm

            if phases == 5 and hf == 0:
                wait("sync", sc, t_g1done)
                tl = dma("sync", S["sout"], taps["d_g1"], B[:, A_G:A_G + 32768])
                wait("sync", S["sout"], tl)
                _finalize(nc, es, EM)
                return nc, taps

            wait("sync", pe, t_h2_mm_last)
            wait("sync", S["pw0"], t_pw_totals[0])
            wait("sync", S["pw1"], t_pw_totals[1])
            t_w3t = 0
            for i4 in range(4):
                t_w3t = dma("sync", S["lnt1"], B[:, A_G + i4 * 8192:A_G + (i4 + 1) * 8192], w3T[i4])
            f5_free = [0, 0, 0, 0]
            t_x1r = [0, 0]
            for tt in range(8):
                xb = tt % 2
                sxr = S["xr0"] if xb == 0 else S["xr1"]
                x1ch = Z[:, Z_STAGE + xb * 1024:Z_STAGE + (xb + 1) * 1024]
                ost = Z[:, Z_SQ:Z_SQ + 1024]
                if t_oadd:
                    wait("sync", ve, t_oadd)
                wait("sync", S["x1w"], t_x1w[14])
                wait("sync", S["x1w1"], t_x1w[15])
                t_x1r[xb] = dma("sync", sxr, x1ch, x1_hbm[tb + tt * 128:tb + (tt + 1) * 128, :])
                for of in range(2):
                    pb_i = (tt * 2 + of) % 4
                    wait("tensor", S["lnt1"], t_w3t)
                    wait("tensor", sc, t_g2done)
                    wait("tensor", ve, f5_free[pb_i])
                    for c in range(32):
                        tmm = emit("tensor", lambda e, pb_i=pb_i, c=c, tt=tt, of=of: e.matmul(
                            PB[pb_i][:, 0:512],
                            HT(c)[:, tt * 128:(tt + 1) * 128],
                            B[:, A_G + c * 1024 + of * 512:A_G + c * 1024 + (of + 1) * 512],
                            start=(c == 0), stop=(c == 31)), pe)
                    t_w3_used = tmm
                    wait("vector", pe, tmm)
                    wait("vector", sxr, t_x1r[xb])
                    if t_ow:
                        wait("vector", S["sout"], t_ow)
                    t_oadd = emit("vector", lambda e, pb_i=pb_i, of=of, x1ch=x1ch, ost=ost: e.tensor_tensor(
                        ost[:, of * 512:(of + 1) * 512],
                        PB[pb_i][:, 0:512],
                        x1ch[:, of * 512:(of + 1) * 512], ALU.add), ve)
                    f5_free[pb_i] = t_oadd
                wait("sync", ve, t_oadd)
                t_ow = dma("sync", S["sout"], out[tb + tt * 128:tb + (tt + 1) * 128, :], ost)
            t_out_last = t_ow

        wait("sync", S["sout"], t_out_last)
        _finalize(nc, es, EM)
        return nc, taps


def _finalize(nc, es, EM):
    block = es.enter_context(nc.Block())
    for eng in ("sync", "gpsimd", "tensor", "vector", "scalar"):
        def runner(e, lst=EM[eng]):
            for f in lst:
                f(e)
        getattr(block, eng)(runner)


_NC_CACHE = {}

def _get_nc(phases=6):
    if phases not in _NC_CACHE:
        _NC_CACHE[phases] = build(phases)
    return _NC_CACHE[phases]


def make_in_maps(inputs):
    xs = np.ascontiguousarray(np.asarray(inputs["L0_attn_x"], dtype=np.float32))
    wq = np.asarray(inputs["L0_attn_Wq"], dtype=np.float32)
    wkv = np.asarray(inputs["L0_attn_Wkv"], dtype=np.float32)
    wo = np.asarray(inputs["L0_attn_Wo"], dtype=np.float32)
    bo = np.asarray(inputs["L0_attn_bo"], dtype=np.float32)
    base = {
        "bo": np.ascontiguousarray(bo),
        "w1": np.ascontiguousarray(np.asarray(inputs["L0_ffn_w1_w"], np.float32)),
        "w2": np.ascontiguousarray(np.asarray(inputs["L0_ffn_w2_w"], np.float32)),
        "h1": np.ascontiguousarray(np.asarray(inputs["L0_ffn_h1_w"], np.float32)),
        "h2": np.ascontiguousarray(np.asarray(inputs["L0_ffn_h2_w"], np.float32)),
        "w3": np.ascontiguousarray(np.asarray(inputs["L0_ffn_w3_w"], np.float32)),
        "s1": np.ascontiguousarray(np.asarray(inputs["L0_ffn_w1_s"], np.float32)),
        "s2": np.ascontiguousarray(np.asarray(inputs["L0_ffn_w2_s"], np.float32)),
        "sh1": np.ascontiguousarray(np.asarray(inputs["L0_ffn_h1_s"], np.float32)),
        "sh2": np.ascontiguousarray(np.asarray(inputs["L0_ffn_h2_s"], np.float32)),
        "s3": np.ascontiguousarray(np.asarray(inputs["L0_ffn_w3_s"], np.float32)),
    }
    in_maps = []
    for core in range(8):
        b, hh = core // 2, core % 2
        m = dict(base)
        m["x"] = np.ascontiguousarray(xs[b])
        m["x_res"] = np.ascontiguousarray(xs[b, hh * T:(hh + 1) * T])
        m["wq"] = np.ascontiguousarray(wq[:, hh * 512:(hh + 1) * 512])
        m["wkv"] = np.ascontiguousarray(
            np.concatenate([wkv[:, hh * 512:(hh + 1) * 512],
                            wkv[:, 1024 + hh * 512:1024 + (hh + 1) * 512]], axis=1))
        m["wo"] = np.ascontiguousarray(wo[hh * 512:(hh + 1) * 512, :])
        in_maps.append(m)
    return in_maps


def kernel(**inputs):
    nc, _taps = _get_nc(6)
    in_maps = make_in_maps(inputs)
    res = run_bass_kernel_spmd(nc, in_maps, core_ids=list(range(8)))
    outp = np.empty((4, N, D), dtype=np.float32)
    for core in range(8):
        b, hh = core // 2, core % 2
        outp[b, hh * T:(hh + 1) * T] = res.results[core]["out"]
    return outp


# version 14
# speedup vs baseline: 1.0586x; 1.0586x over previous
import numpy as np
import contextlib
import concourse.bass as bass
import concourse.mybir as mybir
from concourse.bass_utils import run_bass_kernel_spmd

F32 = mybir.dt.float32
BF16 = mybir.dt.bfloat16
AF = mybir.ActivationFunctionType
ALU = mybir.AluOpType
AX = mybir.AxisListType

import os
NH_EFF = int(os.environ.get("K_NH", "8"))
D = 1024
N = 4096
T = 2048
DHID = 4096
NH = 8
SCALE = 0.125
EPS_LN = 1e-5
EPS_ATT = 1e-8

A_K = 0
A_V = 16384
LNT = 33024
WKV = 41216
WQS = 49408
QST = 53504
KAUG = 33024
QSB = 41216
PT = 49408
AOST = 65792
W3ST = 65920
PSTG = 67968
WOS2 = 70016
AOT = 33024
LN2ST = 53504
A_G = 33024
A2F = 65792
TS1 = 82176
TS2 = 83200
BF_COLS = 84480

Z_STAGE = 0
Z_SQ = 2048
Z_BOBC = 3072
Z_EXTRA = 4096
Z_STATS = 5120
Z_IDENT = 5216
Z_SVEC = 5344
Z_GM = 5480
Z_SMALL = 5512
Z_X1B = 5536
FP_COLS = 7584

SEM_NAMES = ("pe", "ve", "sc", "po", "cc",
             "svec", "sbo", "sx0", "sx1", "ln1w", "lnt0", "lnt1",
             "gin", "gpan", "qw0", "qw1", "qsb", "qsb1", "ao0", "ao1",
             "w3l", "w3w", "yw0", "yw1", "yr", "x1w", "x1w1", "ln2w", "ln1w1", "ln2w1",
             "ff0", "ff1", "xr0", "xr1", "sout",
             "pst0", "pst1", "pw0", "pw1", "gw1", "gw2", "gh1", "gh2")


class Ctr:
    __slots__ = ("h", "v", "name")
    def __init__(self, h, name):
        self.h, self.v, self.name = h, 0, name
    def bump(self, k):
        self.v += k
        return self.v


def build(phases=6):
    nc = bass.Bass("TRN2", target_bir_lowering=False)
    ap = lambda name, shape, dt, kind="ExternalInput": nc.dram_tensor(name, shape, dt, kind=kind).ap()

    x = ap("x", [N, D], F32)
    x_res = ap("x_res", [T, D], F32)
    wq = ap("wq", [D, 512], F32)
    wkv = ap("wkv", [D, D], F32)
    wo = ap("wo", [512, D], F32)
    bo = ap("bo", [D], F32)
    w1 = ap("w1", [DHID, D], F32)
    w2 = ap("w2", [DHID, D], F32)
    h1 = ap("h1", [DHID, DHID], F32)
    h2 = ap("h2", [DHID, DHID], F32)
    w3 = ap("w3", [D, DHID], F32)
    s1 = ap("s1", [DHID], F32)
    s2 = ap("s2", [DHID], F32)
    sh1 = ap("sh1", [DHID], F32)
    sh2 = ap("sh2", [DHID], F32)
    s3 = ap("s3", [D], F32)

    out = ap("out", [T, D], F32, "ExternalOutput")

    ln1_p = ap("ln1_p", [8, N, 128], BF16, "Internal")
    q_hbm = ap("q_hbm", [512, N], BF16, "Internal")
    ao_p = ap("ao_p", [4, N, 128], BF16, "Internal")
    y_int = ap("y_int", [N, D], F32, "Internal")
    y_rs = ap("y_rs", [T, D], F32, "Internal")
    x1_hbm = ap("x1_hbm", [T, D], F32, "Internal")
    ln2_p = ap("ln2_p", [8, T, 128], BF16, "Internal")
    w1p = ap("w1p", [8, DHID, 128], BF16, "Internal")
    w2p = ap("w2p", [8, DHID, 128], BF16, "Internal")
    h1p = ap("h1p", [32, DHID, 128], BF16, "Internal")
    h2p = ap("h2p", [32, DHID, 128], BF16, "Internal")
    w3p = ap("w3p", [32, D, 128], BF16, "Internal")
    w1T = ap("w1T", [16, 128, 2048], BF16, "Internal")
    w2T = ap("w2T", [16, 128, 2048], BF16, "Internal")
    h1T = ap("h1T", [16, 128, 8192], BF16, "Internal")
    h2T = ap("h2T", [16, 128, 8192], BF16, "Internal")
    w3T = ap("w3T", [4, 128, 8192], BF16, "Internal")

    taps = {}
    if phases == 2:
        taps["d_k"] = ap("d_k", [128, 16384], BF16, "ExternalOutput")
        taps["d_v"] = ap("d_v", [128, 16640], BF16, "ExternalOutput")
        taps["d_q"] = ap("d_q", [512, N], BF16, "ExternalOutput")
    elif phases == 3:
        taps["d_ao"] = ap("d_ao", [4, N, 128], BF16, "ExternalOutput")
    elif phases == 3.5:
        taps["d_y"] = ap("d_y", [T, D], F32, "ExternalOutput")
    elif phases == 4:
        taps["d_x1"] = ap("d_x1", [T, D], F32, "ExternalOutput")
        taps["d_ln2"] = ap("d_ln2", [8, T, 128], BF16, "ExternalOutput")
    elif phases == 5:
        taps["d_h"] = ap("d_h", [128, 32768], BF16, "ExternalOutput")
        taps["d_g1"] = ap("d_g1", [128, 32768], BF16, "ExternalOutput")

    EM = {k: [] for k in ("sync", "gpsimd", "tensor", "vector", "scalar")}
    es = contextlib.ExitStack()
    with es:
        B = es.enter_context(nc.sbuf_tensor("abf", [128, BF_COLS], BF16))
        Z = es.enter_context(nc.sbuf_tensor("azf", [128, FP_COLS], F32))
        PB = [es.enter_context(nc.psum_tensor(f"pb{i}", [128, 1024], F32)) for i in range(4)]
        SH = {n: es.enter_context(nc.semaphore(n)) for n in SEM_NAMES}
        S = {n: Ctr(SH[n], n) for n in SEM_NAMES}
        pe, ve, sc, po, cc = (S[n] for n in ("pe", "ve", "sc", "po", "cc"))

        def emit(eng, fn, inc=None, amt=1):
            if inc is None:
                EM[eng].append(fn)
                return None
            t = inc.bump(amt)
            EM[eng].append(lambda e, fn=fn, h=inc.h, a=amt: fn(e).then_inc(h, a))
            return t

        def wait(eng, ctr, v):
            if v:
                EM[eng].append(lambda e, h=ctr.h, v=v: e.wait_ge(h, v))

        def dma(eng, ctr, out_ap, in_ap):
            return emit(eng, lambda e, o=out_ap, i=in_ap: e.dma_start(out=o, in_=i), ctr, 16)

        def dmat(ctr, out_ap, in_ap):
            return emit("sync", lambda e, o=out_ap, i=in_ap: e.dma_start_transpose(o, i), ctr, 16)

        t_id0 = emit("gpsimd", lambda e: e.memset(Z[:, Z_IDENT:Z_IDENT + 128], 1.0), po)
        wait("gpsimd", po, t_id0)
        t_ident = emit("gpsimd", lambda e: e.affine_select(
            Z[:, Z_IDENT:Z_IDENT + 128], Z[:, Z_IDENT:Z_IDENT + 128],
            [[1, 128]], ALU.is_equal, 0.0, base=0, channel_multiplier=-1), po)
        t_ones1 = emit("gpsimd", lambda e: e.memset(Z[0:1, Z_EXTRA:Z_EXTRA + 128], 1.0), po)
        t_epsln = emit("gpsimd", lambda e: e.memset(Z[:, Z_SMALL + 16:Z_SMALL + 17], EPS_LN), po)

        t_prev_copy = 0
        t_prev_mm = 0
        for src, rows, off in ((s1, 32, 0), (s2, 32, 32), (sh1, 32, 64), (sh2, 32, 96), (s3, 8, 128)):
            wait("sync", pe, t_prev_mm)
            tld = dma("sync", S["svec"], Z[0:rows, Z_SQ:Z_SQ + 128],
                      src.rearrange("(c p) -> c p", p=128))
            wait("tensor", S["svec"], tld)
            wait("tensor", po, t_ident)
            wait("tensor", ve, t_prev_copy)
            t_prev_mm = emit("tensor", lambda e, r=rows: e.transpose(
                PB[2][:, 0:r], Z[0:r, Z_SQ:Z_SQ + 128], Z[0:r, Z_IDENT:Z_IDENT + r]), pe)
            wait("vector", pe, t_prev_mm)
            t_prev_copy = emit("vector", lambda e, r=rows, o=off: e.tensor_copy(
                Z[:, Z_SVEC + o:Z_SVEC + o + r], PB[2][:, 0:r]), ve)
        t_svec = t_prev_copy
        t_svec_pe = t_prev_mm

        t_bo_ld = dma("sync", S["sbo"], Z[0:1, Z_BOBC:Z_BOBC + 1024],
                      bo.rearrange("(o d) -> o d", o=1))
        wait("tensor", S["sbo"], t_bo_ld)
        wait("tensor", po, t_ones1)
        wait("tensor", ve, t_svec)
        t_bomm = 0
        for of in range(2):
            t_bomm = emit("tensor", lambda e, of=of: e.matmul(
                PB[2][:, of * 512:(of + 1) * 512], Z[0:1, Z_EXTRA:Z_EXTRA + 128],
                Z[0:1, Z_BOBC + of * 512:Z_BOBC + (of + 1) * 512], start=True, stop=True), pe)
        wait("vector", pe, t_bomm)
        t_bobc = emit("vector", lambda e: e.tensor_copy(Z[:, Z_BOBC:Z_BOBC + 1024], PB[2][:, 0:1024]), ve)

        for ic in range(8):
            dma("gpsimd", S["gin"], B[:, WKV + ic * 1024:WKV + (ic + 1) * 1024],
                wkv[ic * 128:(ic + 1) * 128, :])
        for ic in range(8):
            t_gin_all = dma("gpsimd", S["gin"], B[:, WQS + ic * 512:WQS + (ic + 1) * 512],
                            wq[ic * 128:(ic + 1) * 128, :])
        for c in range(4):
            t_gpan_all = dma("gpsimd", S["gpan"], B[:, WOS2 + c * 1024:WOS2 + (c + 1) * 1024],
                             wo[c * 128:(c + 1) * 128, :])

        def emit_ln(xs, dst_bf, par, sq_off):
            so_ = Z_STATS + par * 12
            ssum = Z[:, so_:so_ + 1]; ssq = Z[:, so_ + 1:so_ + 2]
            mean = Z[:, so_ + 2:so_ + 3]; m2 = Z[:, so_ + 3:so_ + 4]
            var = Z[:, so_ + 4:so_ + 5]; sd = Z[:, so_ + 5:so_ + 6]
            rstd = Z[:, so_ + 6:so_ + 7]; negmr = Z[:, so_ + 7:so_ + 8]
            sq = Z[:, sq_off:sq_off + 1024]
            emit("vector", lambda e: e.reduce_sum(ssum, xs, axis=AX.X))
            t_sqw = emit("scalar", lambda e: e.activation(sq, xs, AF.Square, bias=0.0, scale=1.0), sc)
            wait("vector", sc, t_sqw)
            t_ssq = emit("vector", lambda e: e.reduce_sum(ssq, sq, axis=AX.X), ve)
            wait("vector", ve, t_ssq)
            emit("vector", lambda e: e.tensor_single_scalar(mean, ssum, 1.0 / D, ALU.mult))
            t2 = emit("vector", lambda e: e.tensor_single_scalar(var, ssq, 1.0 / D, ALU.mult), ve)
            wait("vector", ve, t2)
            t_m = emit("vector", lambda e: e.tensor_tensor(m2, mean, mean, ALU.mult), ve)
            wait("vector", ve, t_m)
            t_v2 = emit("vector", lambda e: e.tensor_tensor(var, var, m2, ALU.subtract), ve)
            wait("scalar", ve, t_v2)
            t_sd = emit("scalar", lambda e: e.activation(
                sd, var, AF.Sqrt, bias=Z[:, Z_SMALL + 16:Z_SMALL + 17], scale=1.0), sc)
            wait("vector", sc, t_sd)
            t_r = emit("vector", lambda e: e.reciprocal(rstd, sd), ve)
            wait("vector", ve, t_r)
            t_n = emit("vector", lambda e: e.tensor_scalar(negmr, mean, rstd, -1.0, ALU.mult, ALU.mult), ve)
            wait("vector", ve, t_n)
            t_norm = emit("vector", lambda e: e.tensor_scalar(
                dst_bf, xs, rstd, negmr, ALU.mult, ALU.add), ve)
            return t_norm, t_ssq

        t_norm1 = [0] * 32
        t_pan1 = 0
        t_sq_user = 0
        for c in range(32):
            pbuf = c % 2
            sxc = S["sx0"] if pbuf == 0 else S["sx1"]
            xs = Z[:, Z_STAGE + pbuf * 1024:Z_STAGE + (pbuf + 1) * 1024]
            lns = B[:, 61696 + pbuf * 1024:61696 + (pbuf + 1) * 1024]
            if c >= 2:
                wait("sync", ve, t_norm1[c - 2])
            t_ld = dma("sync", sxc, xs, x[c * 128:(c + 1) * 128, :])
            wait("vector", sxc, t_ld)
            wait("scalar", sxc, t_ld)
            if c == 0:
                wait("scalar", po, t_epsln)
                wait("scalar", pe, t_svec_pe)
            if c >= 1:
                wait("scalar", ve, t_sq_user)
            spn = S["ln1w"] if pbuf == 0 else S["ln1w1"]
            if c >= 2:
                wait("vector", spn, 16 * 8 * (c // 2))
            t_norm1[c], t_sq_user = emit_ln(xs, lns, pbuf, Z_SQ)
            wait("sync", ve, t_norm1[c])
            if c >= 2:
                wait("sync", spn, 16 * 8 * (c // 2))
            for cc_ in range(8):
                t_pan1 = dma("sync", spn, ln1_p[cc_, c * 128:(c + 1) * 128, :],
                             lns[:, cc_ * 128:(cc_ + 1) * 128])
        t_ln1_totals = (S["ln1w"].v, S["ln1w1"].v)

        KT = lambda it: B[:, A_K + it * 4096:A_K + (it + 1) * 4096]
        def vaug(kc):
            return B[:, A_V + kc * 520:A_V + (kc + 1) * 520].rearrange("p (h d) -> p h d", d=65)
        t_vones = 0
        for kc in range(32):
            t_vones = emit("gpsimd", lambda e, kc=kc: e.memset(vaug(kc)[:, :, 64:65], 1.0), po)

        t_lnT_used = [0, 0]
        pk_free = [0, 0, 0, 0]
        t_kevac_all = 0
        t_vevac_all = 0
        t_qw = [0, 0]
        t_qevac = [0, 0]
        t_p2_mm_last = 0
        t_lnld = [0, 0]
        for s in range(8):
            lb = s % 2
            slt = S["lnt0"] if lb == 0 else S["lnt1"]
            lnT = B[:, LNT + lb * 4096:LNT + (lb + 1) * 4096]
            wait("sync", pe, t_lnT_used[lb])
            wait("sync", S["ln1w"], min(16 * 8 * (2 * s + 2), t_ln1_totals[0]))
            wait("sync", S["ln1w1"], min(16 * 8 * (2 * s + 2), t_ln1_totals[1]))
            for ic in range(8):
                t_lnld[lb] = dmat(slt, lnT[:, ic * 512:(ic + 1) * 512],
                                  ln1_p[ic, s * 512:(s + 1) * 512, :])
            for it in range(4):
                pb_i = it % 2
                wait("tensor", slt, t_lnld[lb])
                wait("tensor", S["gin"], t_gin_all)
                wait("tensor", sc, pk_free[pb_i])
                for ic in range(8):
                    tm = emit("tensor", lambda e, pb_i=pb_i, ic=ic, it=it, lnT=lnT: e.matmul(
                        PB[pb_i][:, 0:512],
                        B[:, WKV + ic * 1024 + it * 128:WKV + ic * 1024 + (it + 1) * 128],
                        lnT[:, ic * 512:(ic + 1) * 512],
                        start=(ic == 0), stop=(ic == 7)), pe)
                wait("scalar", pe, tm)
                t_kevac_all = emit("scalar", lambda e, it=it, s=s, pb_i=pb_i: e.activation(
                    KT(it)[:, s * 512:(s + 1) * 512], PB[pb_i][:, 0:512],
                    AF.Copy, bias=0.0, scale=1.0), sc)
                pk_free[pb_i] = t_kevac_all
            for ks in range(4):
                kc = s * 4 + ks
                pb_i = 2 + ks % 2
                wait("tensor", sc, pk_free[pb_i])
                for ic in range(8):
                    tm = emit("tensor", lambda e, pb_i=pb_i, ic=ic, ks=ks, lnT=lnT: e.matmul(
                        PB[pb_i][:, 0:512],
                        lnT[:, ic * 512 + ks * 128:ic * 512 + (ks + 1) * 128],
                        B[:, WKV + ic * 1024 + 512:WKV + (ic + 1) * 1024],
                        start=(ic == 0), stop=(ic == 7)), pe)
                wait("scalar", pe, tm)
                wait("scalar", po, t_vones)
                t_vevac_all = emit("scalar", lambda e, kc=kc, pb_i=pb_i: e.activation(
                    vaug(kc)[:, 0:8, 0:64], PB[pb_i][:, 0:512],
                    AF.Copy, bias=0.0, scale=1.0), sc)
                pk_free[pb_i] = t_vevac_all
            for it in range(4):
                pb_i = it % 2
                qb = it % 2
                sqw = S["qw0"] if qb == 0 else S["qw1"]
                wait("tensor", sc, pk_free[pb_i])
                wait("tensor", ve, t_qevac[qb])
                for ic in range(8):
                    tm = emit("tensor", lambda e, pb_i=pb_i, ic=ic, it=it, lnT=lnT: e.matmul(
                        PB[pb_i][:, 0:512],
                        B[:, WQS + ic * 512 + it * 128:WQS + ic * 512 + (it + 1) * 128],
                        lnT[:, ic * 512:(ic + 1) * 512],
                        start=(ic == 0), stop=(ic == 7)), pe)
                t_p2_mm_last = tm
                wait("scalar", pe, tm)
                wait("scalar", sqw, t_qw[qb])
                t_qevac[qb] = emit("scalar", lambda e, qb=qb, pb_i=pb_i: e.activation(
                    B[:, QST + qb * 512:QST + (qb + 1) * 512], PB[pb_i][:, 0:512],
                    AF.Copy, bias=0.0, scale=SCALE), sc)
                pk_free[pb_i] = t_qevac[qb]
                wait("sync", sc, t_qevac[qb])
                t_qw[qb] = dma("sync", sqw, q_hbm[it * 128:(it + 1) * 128, s * 512:(s + 1) * 512],
                               B[:, QST + qb * 512:QST + (qb + 1) * 512])
            t_lnT_used[lb] = t_p2_mm_last

        if phases == 2:
            wait("sync", sc, t_kevac_all)
            wait("sync", sc, t_vevac_all)
            dma("sync", S["sout"], taps["d_k"], B[:, A_K:A_K + 16384])
            dma("sync", S["sout"], taps["d_v"], B[:, A_V:A_V + 16640])
            wait("sync", S["qw0"], t_qw[0])
            wait("sync", S["qw1"], t_qw[1])
            t3 = dma("sync", S["sout"], taps["d_q"], q_hbm)
            wait("sync", S["sout"], t3)
            _finalize(nc, es, EM)
            return nc, taps

        wait("gpsimd", pe, t_p2_mm_last)
        t_kaug_ones = emit("gpsimd", lambda e: e.memset(B[64:65, KAUG:KAUG + 8192], 1.0), po)
        cast_jobs = ([("w1", w1p, w1, c) for c in range(8)] +
                     [("w2", w2p, w2, c) for c in range(8)] +
                     [("h1", h1p, h1, c) for c in range(32)] +
                     [("h2", h2p, h2, c) for c in range(32)])
        cast_state = {"idx": 0}
        cast_sem = {"w1": "gw1", "w2": "gw2", "h1": "gh1", "h2": "gh2"}

        def emit_cast(n):
            for _ in range(n):
                k = cast_state["idx"]
                if k >= len(cast_jobs):
                    return
                cast_state["idx"] = k + 1
                nm, wp_, wsrc, c = cast_jobs[k]
                dma("gpsimd", S[cast_sem[nm]], wp_[c], wsrc[:, c * 128:(c + 1) * 128])
                if k + 1 >= len(cast_jobs) or cast_jobs[k + 1][0] != nm:
                    cast_totals[nm] = S[cast_sem[nm]].v
        pv_ps = [PB[3][:, 0:65], PB[3][:, 512:577]]
        p2s = lambda r: PB[2][:, r * 512:(r + 1) * 512]

        pT_lastread = [0] * 32
        t_kaug = [0, 0]
        t_pass2_last = [0] * NH
        t_gmrow = [0] * 32
        t_comb = [0] * 32
        p1_free = [0, 0]
        p2_free = [0, 0, 0, 0]
        t_exp = {}
        t_mask = {}
        t_pv_free = [0, 0]
        t_pv_l = [0, 0]
        t_ao_w = [0, 0]
        t_qload = [0, 0]
        gcount = 0

        w3_jobs = [(blk, hf2) for blk in range(8) for hf2 in range(2)]
        w3_state = {"stage_free": 0, "last_w": 0}
        t_w3_total = 0

        def emit_w3(jix):
            nonlocal t_w3_total
            blk, hf2 = w3_jobs[jix]
            zst = Z[:, Z_STAGE:Z_STAGE + 2048]
            if jix == 0:
                wait("sync", ve, t_norm1[31])
                wait("scalar", S["qw0"], t_qw[0])
                wait("scalar", S["qw1"], t_qw[1])
            wait("sync", sc, w3_state["stage_free"])
            tld = dma("sync", S["w3l"], zst, w3[blk * 128:(blk + 1) * 128, hf2 * 2048:(hf2 + 1) * 2048])
            wait("scalar", S["w3l"], tld)
            wait("scalar", ve, t_svec)
            wait("scalar", S["w3w"], w3_state["last_w"])
            tc = emit("scalar", lambda e, blk=blk: e.activation(
                B[:, W3ST:W3ST + 2048], zst, AF.Copy, bias=0.0,
                scale=Z[:, Z_SVEC + 128 + blk:Z_SVEC + 128 + blk + 1]), sc)
            w3_state["stage_free"] = tc
            wait("sync", sc, tc)
            tw = 0
            for cc_ in range(16):
                hc = hf2 * 16 + cc_
                tw = dma("sync", S["w3w"], w3p[hc, blk * 128:(blk + 1) * 128, :],
                         B[:, W3ST + cc_ * 128:W3ST + (cc_ + 1) * 128])
            w3_state["last_w"] = tw
            t_w3_total = tw

        P3PART = int(os.environ.get("K_P3PART", "30"))
        prep_jobs = []
        for nm_, wp_, wT_, npan in (("w1", w1p, w1T, 8), ("w2", w2p, w2T, 8),
                                    ("h1", h1p, h1T, 32), ("h2", h2p, h2T, 32)):
            for p_ in range(npan):
                for r_ in range(4):
                    prep_jobs.append((nm_, wp_, wT_, p_, r_, npan))
        prep_state = {"idx": 0, "wr": [(None, 0), (None, 0)], "gated": set()}
        cast_totals = {}

        def emit_prep(n):
            for _ in range(n):
                k = prep_state["idx"]
                if k >= len(prep_jobs):
                    return
                prep_state["idx"] = k + 1
                nm_, wp_, wT_, p_, r_, npan = prep_jobs[k]
                bufp = k % 2
                spst = S["pst0"] if bufp == 0 else S["pst1"]
                spw = S["pw0"] if bufp == 0 else S["pw1"]
                stg = B[:, PSTG + bufp * 1024:PSTG + (bufp + 1) * 1024]
                if nm_ not in prep_state["gated"]:
                    while nm_ not in cast_totals:
                        emit_cast(1)
                    wait("sync", S[cast_sem[nm_]], cast_totals[nm_])
                    prep_state["gated"].add(nm_)
                ws, wt = prep_state["wr"][bufp]
                if ws is not None:
                    wait("sync", ws, wt)
                tld = dmat(spst, stg, wp_[p_, r_ * 1024:(r_ + 1) * 1024, :])
                wait("sync", spst, tld)
                tw = 0
                for q_ in range(4):
                    tw = dma("sync", spw, wT_[r_ * 4 + q_][:, p_ * 256:(p_ + 1) * 256],
                             stg[:, q_ * 256:(q_ + 1) * 256])
                prep_state["wr"][bufp] = (spw, tw)

        prev_comb = [0] * 32

        def gen_pass1(i):
            nonlocal gcount
            hcur = i
            kb = hcur % 2
            qsb_o = QSB + (hcur % 2) * 4096
            for qq in range(32):
                par = qq % 2
                pcols = Z_SMALL + par * 4
                if qq >= 2:
                    wait("vector", ve, t_comb[qq - 2])
                elif i > 0:
                    wait("vector", ve, prev_comb[30 + qq])
                t_r = 0
                for g in range(4):
                    pb_i = gcount % 2
                    gcount += 1
                    wait("tensor", S["qsb"] if hcur % 2 == 0 else S["qsb1"], t_qload[hcur % 2])
                    wait("tensor", ve, t_kaug[kb])
                    wait("tensor", po, t_kaug_ones)
                    wait("tensor", ve, p1_free[pb_i])
                    for sub in range(2):
                        tmm = emit("tensor", lambda e, pb_i=pb_i, qq=qq, g=g, sub=sub, kb=kb, qsb_o=qsb_o: e.matmul(
                            PB[pb_i][:, sub * 512:(sub + 1) * 512],
                            B[0:64, qsb_o + qq * 128:qsb_o + (qq + 1) * 128],
                            B[0:64, KAUG + kb * 4096 + (g * 2 + sub) * 512:KAUG + kb * 4096 + (g * 2 + sub + 1) * 512],
                            start=True, stop=True), pe)
                    wait("vector", pe, tmm)
                    t_r = emit("vector", lambda e, pcols=pcols, g=g, pb_i=pb_i: e.reduce_max(
                        Z[:, pcols + g:pcols + g + 1], PB[pb_i][:, 0:1024], axis=AX.X), ve)
                    p1_free[pb_i] = t_r
                    yield
                wait("vector", ve, t_r)
                t_comb[qq] = emit("vector", lambda e, pcols=pcols, qq=qq: e.reduce_max(
                    Z[:, Z_GM + qq:Z_GM + qq + 1], Z[:, pcols:pcols + 4],
                    axis=AX.X, negate=True), ve)

        def gen_work(h):
            qsb_h = QSB + (h % 2) * 4096
            wait("tensor", ve, prev_comb[31])
            wait("tensor", sc, p2_free[0])
            for qq in range(32):
                if qq > 0:
                    wait("tensor", ve, t_gmrow[qq - 1])
                tmm = emit("tensor", lambda e, qq=qq: e.transpose(
                    PB[2][0:1, 0:128], Z[:, Z_GM + qq:Z_GM + qq + 1],
                    Z[:, Z_IDENT:Z_IDENT + 128]), pe)
                wait("vector", pe, tmm)
                t_gmrow[qq] = emit("vector", lambda e, qq=qq, qsb_h=qsb_h: e.tensor_copy(
                    B[64:65, qsb_h + qq * 128:qsb_h + (qq + 1) * 128], PB[2][0:1, 0:128]), ve)
                if qq % 2 == 1:
                    yield
            for j in range(8):
                g0 = j * 512
                kcs = list(range(4 * j + 4))
                for gbase in range(0, len(kcs), 2):
                    grp = kcs[gbase:gbase + 2]
                    wait("tensor", sc, p2_free[0])
                    wait("tensor", ve, t_gmrow[31])
                    wait("tensor", ve, t_kaug[h % 2])
                    wait("tensor", po, t_kaug_ones)
                    for gi, kc in enumerate(grp):
                        tmm = emit("tensor", lambda e, kc=kc, j=j, r=gi, kb=h % 2, qsb_h=qsb_h: e.matmul(
                            p2s(r),
                            B[0:65, KAUG + kb * 4096 + kc * 128:KAUG + kb * 4096 + (kc + 1) * 128],
                            B[0:65, qsb_h + j * 512:qsb_h + (j + 1) * 512],
                            start=True, stop=True), pe if gi == len(grp) - 1 else None)
                    t_pass2_last[h] = tmm
                    wait("scalar", pe, tmm)
                    wait("scalar", pe, max(pT_lastread[kc] for kc in grp))
                    for gi, kc in enumerate(grp):
                        texp = emit("scalar", lambda e, kc=kc, r=gi: e.activation(
                            B[:, PT + kc * 512:PT + (kc + 1) * 512], p2s(r),
                            AF.Exp, bias=0.0, scale=1.0), sc if gi == len(grp) - 1 else None)
                    for kc in grp:
                        t_exp[kc] = texp
                    p2_free[0] = texp
                    for kc in grp:
                        if kc >= 4 * j:
                            wait("gpsimd", sc, texp)
                            t_mask[kc] = emit("gpsimd", lambda e, kc=kc, bs=g0 - kc * 128: e.affine_select(
                                B[:, PT + kc * 512:PT + (kc + 1) * 512],
                                B[:, PT + kc * 512:PT + (kc + 1) * 512],
                                [[1, 512]], ALU.is_ge, 0.0, base=bs, channel_multiplier=-1), po)
                        else:
                            t_mask[kc] = None
                    yield
                for rr_ in range(4):
                    qq = 4 * j + rr_
                    kl = list(range(qq + 1))
                    pvb = qq % 2
                    pv = pv_ps[pvb]
                    wait("tensor", sc, t_pv_free[pvb])
                    wait("tensor", ve, t_pv_l[pvb])
                    nmm = 0
                    for idx, kc in enumerate(kl):
                        wait("tensor", sc, t_exp[kc])
                        if t_mask[kc] is not None:
                            wait("tensor", po, t_mask[kc])
                        tmm = emit("tensor", lambda e, pv=pv, kc=kc, rr_=rr_, hh2=h,
                                   st=(idx == 0), sp=(idx == len(kl) - 1): e.matmul(
                            pv,
                            B[:, PT + kc * 512 + rr_ * 128:PT + kc * 512 + rr_ * 128 + 128],
                            B[:, A_V + kc * 520 + hh2 * 65:A_V + kc * 520 + (hh2 + 1) * 65],
                            start=st, stop=sp), pe)
                        pT_lastread[kc] = tmm
                        nmm += 1
                        if nmm % 4 == 0:
                            yield
                    le = Z[:, Z_SMALL + 8 + pvb:Z_SMALL + 9 + pvb]
                    li = Z[:, Z_SMALL + 12 + pvb:Z_SMALL + 13 + pvb]
                    wait("vector", pe, tmm)
                    wait("vector", sc, t_pv_free[pvb])
                    tv1 = emit("vector", lambda e, le=le, pv=pv: e.tensor_single_scalar(
                        le, pv[:, 64:65], EPS_ATT, ALU.add), ve)
                    wait("vector", ve, tv1)
                    tv2 = emit("vector", lambda e, li=li, le=le: e.reciprocal(li, le), ve)
                    t_pv_l[pvb] = tv2
                    aob = qq % 2
                    sao = S["ao0"] if aob == 0 else S["ao1"]
                    aos = B[:, AOST + aob * 64:AOST + (aob + 1) * 64]
                    wait("scalar", ve, tv2)
                    wait("scalar", sao, t_ao_w[aob])
                    tse = emit("scalar", lambda e, aos=aos, pv=pv, li=li: e.activation(
                        aos, pv[:, 0:64], AF.Copy, bias=0.0, scale=li), sc)
                    t_pv_free[pvb] = tse
                    wait("sync", sc, tse)
                    t_ao_w[aob] = dma("sync", sao,
                                      ao_p[h // 2, qq * 128:(qq + 1) * 128,
                                           (h % 2) * 64:(h % 2) * 64 + 64], aos)
                    yield

        for i in range(NH_EFF + 1):
            h = i - 1
            prev_comb = list(t_comb)
            if i < NH_EFF:
                hcur = i
                kb = hcur % 2
                it = hcur // 2
                rbase = (hcur % 2) * 64
                qsb_o = QSB + (hcur % 2) * 4096
                wait("vector", sc, t_kevac_all)
                if i >= 2:
                    wait("vector", pe, t_pass2_last[i - 2])
                else:
                    wait("vector", pe, t_p2_mm_last)
                t_kaug[kb] = emit("vector", lambda e, kb=kb, it=it, rbase=rbase: e.tensor_copy(
                    B[0:64, KAUG + kb * 4096:KAUG + (kb + 1) * 4096],
                    B[rbase:rbase + 64, A_K + it * 4096:A_K + (it + 1) * 4096]), ve)
                if i == 0:
                    wait("sync", pe, t_p2_mm_last)
                    wait("sync", S["qw0"], t_qw[0])
                    wait("sync", S["qw1"], t_qw[1])
                    t_qload[0] = dma("sync", S["qsb"], B[0:64, qsb_o:qsb_o + 4096],
                                     q_hbm[hcur * 64:(hcur + 1) * 64, :])
            ga = gen_pass1(i) if i < NH_EFF else None
            gb = gen_work(h) if h >= 0 else None
            na = nb = True
            ticks = 0
            while (ga is not None and na) or (gb is not None and nb):
                if ga is not None and na:
                    try:
                        next(ga)
                    except StopIteration:
                        na = False
                for _ in range(2):
                    if gb is not None and nb:
                        try:
                            next(gb)
                        except StopIteration:
                            nb = False
                ticks += 1
                if ticks % 12 == 0:
                    emit_cast(1)
                if i >= 3 and ticks % 2 == 0:
                    emit_prep(1)
            if h >= 0:
                emit_w3(2 * h)
                emit_w3(2 * h + 1)
            if i + 1 < NH_EFF:
                nx = i + 1
                qsb_n = QSB + (nx % 2) * 4096
                wait("sync", pe, t_pass2_last[nx - 2] if nx >= 2 else t_p2_mm_last)
                wait("sync", S["qw0"], t_qw[0])
                wait("sync", S["qw1"], t_qw[1])
                sqb_n = S["qsb"] if nx % 2 == 0 else S["qsb1"]
                t_qload[nx % 2] = dma("sync", sqb_n, B[0:64, qsb_n:qsb_n + 4096],
                                      q_hbm[nx * 64:(nx + 1) * 64, :])

        for jix in range(2 * NH, len(w3_jobs)):
            emit_w3(jix)
        emit_cast(len(cast_jobs))
        emit_prep(len(prep_jobs))

        if phases == 3:
            wait("sync", S["ao0"], t_ao_w[0])
            wait("sync", S["ao1"], t_ao_w[1])
            tl = dma("sync", S["sout"], taps["d_ao"], ao_p)
            wait("sync", S["sout"], tl)
            _finalize(nc, es, EM)
            return nc, taps

        t_aot = 0
        wait("sync", S["ao0"], t_ao_w[0])
        wait("sync", S["ao1"], t_ao_w[1])
        for c in range(4):
            t_aot = dmat(S["qsb"], B[:, AOT + c * 4096:AOT + (c + 1) * 4096], ao_p[c])
        y_free = [0, 0]
        t_yw = [0, 0]
        for tt in range(32):
            pb_i = tt % 2
            syw = S["yw0"] if pb_i == 0 else S["yw1"]
            yst = Z[:, Z_STAGE + pb_i * 1024:Z_STAGE + (pb_i + 1) * 1024]
            wait("tensor", S["qsb"], t_aot)
            wait("tensor", S["gpan"], t_gpan_all)
            wait("tensor", ve, y_free[pb_i])
            for of in range(2):
                for c in range(4):
                    tmm = emit("tensor", lambda e, pb_i=pb_i, of=of, c=c, tt=tt: e.matmul(
                        PB[pb_i][:, of * 512:(of + 1) * 512],
                        B[:, AOT + c * 4096 + tt * 128:AOT + c * 4096 + (tt + 1) * 128],
                        B[:, WOS2 + c * 1024 + of * 512:WOS2 + c * 1024 + (of + 1) * 512],
                        start=(c == 0), stop=(c == 3)), pe)
            wait("vector", pe, tmm)
            wait("vector", syw, t_yw[pb_i])
            t_ycp = emit("vector", lambda e, yst=yst, pb_i=pb_i: e.tensor_copy(yst, PB[pb_i][:, 0:1024]), ve)
            y_free[pb_i] = t_ycp
            wait("sync", ve, t_ycp)
            t_yw[pb_i] = dma("sync", syw, y_int[tt * 128:(tt + 1) * 128, :], yst)

        for hc in range(32):
            bufp = hc % 2
            spst = S["pst0"] if bufp == 0 else S["pst1"]
            spw = S["pw0"] if bufp == 0 else S["pw1"]
            stg = B[:, PSTG + bufp * 1024:PSTG + (bufp + 1) * 1024]
            if hc == 0:
                wait("sync", S["w3w"], t_w3_total)
            ws, wt = prep_state["wr"][bufp]
            if ws is not None:
                wait("sync", ws, wt)
            tld = dmat(spst, stg, w3p[hc])
            wait("sync", spst, tld)
            tw = dma("sync", spw, w3T[hc // 8][:, (hc % 8) * 1024:(hc % 8 + 1) * 1024], stg)
            prep_state["wr"][bufp] = (spw, tw)
        t_pw_totals = (S["pw0"].v, S["pw1"].v)

        wait("gpsimd", S["yw0"], t_yw[0])
        wait("gpsimd", S["yw1"], t_yw[1])
        t_cc = emit("gpsimd", lambda e: e.collective_compute(
            "ReduceScatter", ALU.add,
            replica_groups=[[0, 1], [2, 3], [4, 5], [6, 7]],
            ins=[y_int.opt()], outs=[y_rs.opt()]), cc, 1)

        if phases == 3.5:
            wait("sync", cc, t_cc)
            tl = dma("sync", S["sout"], taps["d_y"], y_rs)
            wait("sync", S["sout"], tl)
            _finalize(nc, es, EM)
            return nc, taps

        t_norm2 = [0] * 16
        t_pan2 = 0
        t_x1w = [0] * 16
        t_add2 = 0
        t_sq_user2 = 0
        for t_ in range(16):
            ych = Z[:, Z_STAGE:Z_STAGE + 1024]
            xrch = Z[:, Z_STAGE + 1024:Z_STAGE + 2048]
            x1ch = Z[:, Z_X1B + (t_ % 2) * 1024:Z_X1B + (t_ % 2 + 1) * 1024]
            if t_ >= 1:
                wait("sync", ve, t_add2)
            if t_ >= 2:
                wait("sync", ve, t_norm2[t_ - 2])
                wait("sync", S["x1w"] if t_ % 2 == 0 else S["x1w1"], t_x1w[t_ - 2])
            wait("sync", cc, t_cc)
            dma("sync", S["yr"], ych, y_rs[t_ * 128:(t_ + 1) * 128, :])
            t_l2 = dma("sync", S["yr"], xrch, x_res[t_ * 128:(t_ + 1) * 128, :])
            wait("vector", S["yr"], t_l2)
            wait("vector", ve, t_bobc)
            if t_ >= 2:
                wait("vector", ve, t_norm2[t_ - 2])
            if t_ >= 1:
                wait("vector", ve, t_sq_user2)
            t_a1 = emit("vector", lambda e, ych=ych, xrch=xrch, x1ch=x1ch: e.tensor_tensor(
                x1ch, ych, xrch, ALU.add), ve)
            wait("vector", ve, t_a1)
            t_add2 = emit("vector", lambda e, x1ch=x1ch: e.tensor_tensor(
                x1ch, x1ch, Z[:, Z_BOBC:Z_BOBC + 1024], ALU.add), ve)
            wait("vector", ve, t_add2)
            wait("scalar", ve, t_add2)
            ln2s = B[:, LN2ST:LN2ST + 1024]
            spn2 = S["ln2w"] if t_ % 2 == 0 else S["ln2w1"]
            oth2 = S["ln2w1"] if t_ % 2 == 0 else S["ln2w"]
            if t_ >= 1:
                wait("vector", oth2, 16 * 8 * ((t_ + 1) // 2))
            if t_ >= 2:
                wait("vector", spn2, 16 * 8 * (t_ // 2))
            t_norm2[t_], t_sq_user2 = emit_ln(x1ch, ln2s, t_ % 2, Z_EXTRA)
            wait("sync", ve, t_norm2[t_])
            t_x1w[t_] = dma("sync", S["x1w"] if t_ % 2 == 0 else S["x1w1"],
                            x1_hbm[t_ * 128:(t_ + 1) * 128, :], x1ch)
            for cc_ in range(8):
                t_pan2 = dma("sync", spn2, ln2_p[cc_, t_ * 128:(t_ + 1) * 128, :],
                             ln2s[:, cc_ * 128:(cc_ + 1) * 128])
        t_ln2_totals = (S["ln2w"].v, S["ln2w1"].v)

        if phases == 4:
            wait("sync", S["x1w"], t_x1w[14])
            wait("sync", S["x1w1"], t_x1w[15])
            wait("sync", S["ln2w"], t_ln2_totals[0])
            wait("sync", S["ln2w1"], t_ln2_totals[1])
            dma("sync", S["sout"], taps["d_x1"], x1_hbm)
            tl = dma("sync", S["sout"], taps["d_ln2"], ln2_p)
            wait("sync", S["sout"], tl)
            _finalize(nc, es, EM)
            return nc, taps

        HT = lambda c: B[:, A_K + c * 1024:A_K + (c + 1) * 1024]
        GT = lambda c: B[:, A_G + c * 1024:A_G + (c + 1) * 1024]
        t_out_last = 0
        t_w3_used = 0
        t_h2_mm_last = 0
        t_ow = 0
        t_oadd = 0
        for hf in range(2):
            tb = hf * 1024
            if hf == 0:
                wait("sync", S["ln2w"], 16 * 8 * 4)
                wait("sync", S["ln2w1"], 16 * 8 * 4)
            else:
                wait("sync", S["ln2w"], t_ln2_totals[0])
                wait("sync", S["ln2w1"], t_ln2_totals[1])
            if hf == 1:
                wait("sync", pe, t_h2_mm_last)
                wait("sync", pe, t_w3_used)
            if hf == 0:
                wait("sync", S["pw0"], t_pw_totals[0])
                wait("sync", S["pw1"], t_pw_totals[1])
                wait("sync", S["pst0"], S["pst0"].v)
                wait("sync", S["pst1"], S["pst1"].v)
            t_lnT2 = 0
            for c in range(8):
                t_lnT2 = dmat(S["lnt0"], B[:, A2F + c * 1024:A2F + (c + 1) * 1024],
                              ln2_p[c, tb:tb + 1024, :])
            t_w1t = [0, 0]
            t_wstream_used = [0, 0]
            f2_free = [0, 0]
            t_h_evt = 0
            t_ts_free = 0
            for hb in range(16):
                wb = hb % 2
                sff = S["ff0"] if wb == 0 else S["ff1"]
                W1T = A2F + 8192 + wb * 2048
                W2T = A2F + 12288 + wb * 2048
                wait("sync", pe, t_wstream_used[wb])
                if hf == 0 and hb < 2:
                    wait("sync", S["pw0"], t_pw_totals[0])
                    wait("sync", S["pw1"], t_pw_totals[1])
                dma("sync", sff, B[:, W1T:W1T + 2048], w1T[hb])
                t_w1t[wb] = dma("sync", sff, B[:, W2T:W2T + 2048], w2T[hb])
                for sub in range(2):
                    hrow = hb * 2 + sub
                    for tg in range(2):
                        pb_i = (hrow * 2 + tg) % 2
                        wait("tensor", sff, t_w1t[wb])
                        wait("tensor", S["lnt0"], t_lnT2)
                        wait("tensor", ve, f2_free[pb_i])
                        for c in range(8):
                            emit("tensor", lambda e, pb_i=pb_i, c=c, sub=sub, tg=tg, W1T=W1T: e.matmul(
                                PB[pb_i][:, 0:512],
                                B[:, W1T + c * 256 + sub * 128:W1T + c * 256 + (sub + 1) * 128],
                                B[:, A2F + c * 1024 + tg * 512:A2F + c * 1024 + (tg + 1) * 512],
                                start=(c == 0), stop=(c == 7)))
                        for c in range(8):
                            tmm = emit("tensor", lambda e, pb_i=pb_i, c=c, sub=sub, tg=tg, W2T=W2T: e.matmul(
                                PB[pb_i][:, 512:1024],
                                B[:, W2T + c * 256 + sub * 128:W2T + c * 256 + (sub + 1) * 128],
                                B[:, A2F + c * 1024 + tg * 512:A2F + c * 1024 + (tg + 1) * 512],
                                start=(c == 0), stop=(c == 7)), pe)
                        t_wstream_used[wb] = tmm
                        wait("scalar", pe, tmm)
                        wait("scalar", ve, t_ts_free)
                        tb_i = tg
                        sv = Z[:, Z_SVEC + hrow:Z_SVEC + hrow + 1]
                        sv2 = Z[:, Z_SVEC + 32 + hrow:Z_SVEC + 32 + hrow + 1]
                        if hf == 0 and hb == 0 and sub == 0 and tg == 0:
                            wait("scalar", ve, t_svec)
                        emit("scalar", lambda e, pb_i=pb_i, tb_i=tb_i, sv=sv: e.activation(
                            B[:, TS1 + tb_i * 512:TS1 + (tb_i + 1) * 512], PB[pb_i][:, 0:512],
                            AF.Silu, bias=0.0, scale=sv))
                        t_s2 = emit("scalar", lambda e, pb_i=pb_i, tb_i=tb_i, sv2=sv2: e.activation(
                            B[:, TS2 + tb_i * 512:TS2 + (tb_i + 1) * 512], PB[pb_i][:, 512:1024],
                            AF.Copy, bias=0.0, scale=sv2), sc)
                        wait("vector", sc, t_s2)
                        t_h_evt = emit("vector", lambda e, hrow=hrow, tg=tg, tb_i=tb_i: e.tensor_tensor(
                            HT(hrow)[:, tg * 512:(tg + 1) * 512],
                            B[:, TS1 + tb_i * 512:TS1 + (tb_i + 1) * 512],
                            B[:, TS2 + tb_i * 512:TS2 + (tb_i + 1) * 512], ALU.mult), ve)
                        f2_free[pb_i] = t_h_evt
                        t_ts_free = t_h_evt
            t_hdone = t_h_evt

            if phases == 5 and hf == 0:
                wait("sync", ve, t_hdone)
                dma("sync", S["sout"], taps["d_h"], B[:, A_K:A_K + 32768])

            t_prev_layer_mm = max(t_wstream_used)
            t_g1done = 0
            t_g2done = 0
            for layer in range(2):
                hp = h1p if layer == 0 else h2p
                svoff = 64 if layer == 0 else 96
                src_t = HT if layer == 0 else GT
                dst_t = GT if layer == 0 else HT
                t_ht = [0, 0]
                t_hstream_used = [t_prev_layer_mm, t_prev_layer_mm]
                f3_free = [0, 0]
                t_gevt = 0
                if layer == 0:
                    wait("tensor", ve, t_hdone)
                else:
                    wait("tensor", sc, t_g1done)
                hT_ = h1T if layer == 0 else h2T
                for ob in range(16):
                    wb = ob % 2
                    sff = S["ff0"] if wb == 0 else S["ff1"]
                    HTS = A2F + wb * 8192
                    wait("sync", pe, t_hstream_used[wb])
                    t_ht[wb] = dma("sync", sff, B[:, HTS:HTS + 8192], hT_[ob])
                    for sub in range(2):
                        orow = ob * 2 + sub
                        for tg in range(2):
                            pb_i = 2 + (orow * 2 + tg) % 2
                            wait("tensor", sff, t_ht[wb])
                            wait("tensor", sc, f3_free[pb_i - 2])
                            for c in range(32):
                                tmm = emit("tensor", lambda e, pb_i=pb_i, c=c, sub=sub, tg=tg,
                                           HTS=HTS, src_t=src_t: e.matmul(
                                    PB[pb_i][:, 0:512],
                                    B[:, HTS + c * 256 + sub * 128:HTS + c * 256 + (sub + 1) * 128],
                                    src_t(c)[:, tg * 512:(tg + 1) * 512],
                                    start=(c == 0), stop=(c == 31)), pe)
                            t_hstream_used[wb] = tmm
                            wait("scalar", pe, tmm)
                            sv = Z[:, Z_SVEC + svoff + orow:Z_SVEC + svoff + orow + 1]
                            t_gevt = emit("scalar", lambda e, pb_i=pb_i, orow=orow, tg=tg,
                                          sv=sv, dst_t=dst_t: e.activation(
                                dst_t(orow)[:, tg * 512:(tg + 1) * 512], PB[pb_i][:, 0:512],
                                AF.Gelu, bias=0.0, scale=sv), sc)
                            f3_free[pb_i - 2] = t_gevt
                t_prev_layer_mm = max(t_hstream_used)
                if layer == 0:
                    t_g1done = t_gevt
                else:
                    t_g2done = t_gevt
                    t_h2_mm_last = t_prev_layer_mm

            if phases == 5 and hf == 0:
                wait("sync", sc, t_g1done)
                tl = dma("sync", S["sout"], taps["d_g1"], B[:, A_G:A_G + 32768])
                wait("sync", S["sout"], tl)
                _finalize(nc, es, EM)
                return nc, taps

            wait("sync", pe, t_h2_mm_last)
            wait("sync", S["pw0"], t_pw_totals[0])
            wait("sync", S["pw1"], t_pw_totals[1])
            t_w3t = 0
            for i4 in range(4):
                t_w3t = dma("sync", S["lnt1"], B[:, A_G + i4 * 8192:A_G + (i4 + 1) * 8192], w3T[i4])
            f5_free = [0, 0, 0, 0]
            t_x1r = [0, 0]
            for tt in range(8):
                xb = tt % 2
                sxr = S["xr0"] if xb == 0 else S["xr1"]
                x1ch = Z[:, Z_STAGE + xb * 1024:Z_STAGE + (xb + 1) * 1024]
                ost = Z[:, Z_SQ:Z_SQ + 1024]
                if t_oadd:
                    wait("sync", ve, t_oadd)
                if hf == 0:
                    wait("sync", S["x1w"], t_x1w[6])
                    wait("sync", S["x1w1"], t_x1w[7])
                else:
                    wait("sync", S["x1w"], t_x1w[14])
                    wait("sync", S["x1w1"], t_x1w[15])
                t_x1r[xb] = dma("sync", sxr, x1ch, x1_hbm[tb + tt * 128:tb + (tt + 1) * 128, :])
                for of in range(2):
                    pb_i = (tt * 2 + of) % 4
                    wait("tensor", S["lnt1"], t_w3t)
                    wait("tensor", sc, t_g2done)
                    wait("tensor", ve, f5_free[pb_i])
                    for c in range(32):
                        tmm = emit("tensor", lambda e, pb_i=pb_i, c=c, tt=tt, of=of: e.matmul(
                            PB[pb_i][:, 0:512],
                            HT(c)[:, tt * 128:(tt + 1) * 128],
                            B[:, A_G + c * 1024 + of * 512:A_G + c * 1024 + (of + 1) * 512],
                            start=(c == 0), stop=(c == 31)), pe)
                    t_w3_used = tmm
                    wait("vector", pe, tmm)
                    wait("vector", sxr, t_x1r[xb])
                    if t_ow:
                        wait("vector", S["sout"], t_ow)
                    t_oadd = emit("vector", lambda e, pb_i=pb_i, of=of, x1ch=x1ch, ost=ost: e.tensor_tensor(
                        ost[:, of * 512:(of + 1) * 512],
                        PB[pb_i][:, 0:512],
                        x1ch[:, of * 512:(of + 1) * 512], ALU.add), ve)
                    f5_free[pb_i] = t_oadd
                wait("sync", ve, t_oadd)
                t_ow = dma("sync", S["sout"], out[tb + tt * 128:tb + (tt + 1) * 128, :], ost)
            t_out_last = t_ow

        wait("sync", S["sout"], t_out_last)
        _finalize(nc, es, EM)
        return nc, taps


def _finalize(nc, es, EM):
    block = es.enter_context(nc.Block())
    for eng in ("sync", "gpsimd", "tensor", "vector", "scalar"):
        def runner(e, lst=EM[eng]):
            for f in lst:
                f(e)
        getattr(block, eng)(runner)


_NC_CACHE = {}

def _get_nc(phases=6):
    if phases not in _NC_CACHE:
        _NC_CACHE[phases] = build(phases)
    return _NC_CACHE[phases]


def make_in_maps(inputs):
    xs = np.ascontiguousarray(np.asarray(inputs["L0_attn_x"], dtype=np.float32))
    wq = np.asarray(inputs["L0_attn_Wq"], dtype=np.float32)
    wkv = np.asarray(inputs["L0_attn_Wkv"], dtype=np.float32)
    wo = np.asarray(inputs["L0_attn_Wo"], dtype=np.float32)
    bo = np.asarray(inputs["L0_attn_bo"], dtype=np.float32)
    base = {
        "bo": np.ascontiguousarray(bo),
        "w1": np.ascontiguousarray(np.asarray(inputs["L0_ffn_w1_w"], np.float32)),
        "w2": np.ascontiguousarray(np.asarray(inputs["L0_ffn_w2_w"], np.float32)),
        "h1": np.ascontiguousarray(np.asarray(inputs["L0_ffn_h1_w"], np.float32)),
        "h2": np.ascontiguousarray(np.asarray(inputs["L0_ffn_h2_w"], np.float32)),
        "w3": np.ascontiguousarray(np.asarray(inputs["L0_ffn_w3_w"], np.float32)),
        "s1": np.ascontiguousarray(np.asarray(inputs["L0_ffn_w1_s"], np.float32)),
        "s2": np.ascontiguousarray(np.asarray(inputs["L0_ffn_w2_s"], np.float32)),
        "sh1": np.ascontiguousarray(np.asarray(inputs["L0_ffn_h1_s"], np.float32)),
        "sh2": np.ascontiguousarray(np.asarray(inputs["L0_ffn_h2_s"], np.float32)),
        "s3": np.ascontiguousarray(np.asarray(inputs["L0_ffn_w3_s"], np.float32)),
    }
    in_maps = []
    for core in range(8):
        b, hh = core // 2, core % 2
        m = dict(base)
        m["x"] = np.ascontiguousarray(xs[b])
        m["x_res"] = np.ascontiguousarray(xs[b, hh * T:(hh + 1) * T])
        m["wq"] = np.ascontiguousarray(wq[:, hh * 512:(hh + 1) * 512])
        m["wkv"] = np.ascontiguousarray(
            np.concatenate([wkv[:, hh * 512:(hh + 1) * 512],
                            wkv[:, 1024 + hh * 512:1024 + (hh + 1) * 512]], axis=1))
        m["wo"] = np.ascontiguousarray(wo[hh * 512:(hh + 1) * 512, :])
        in_maps.append(m)
    return in_maps


def kernel(**inputs):
    nc, _taps = _get_nc(6)
    in_maps = make_in_maps(inputs)
    res = run_bass_kernel_spmd(nc, in_maps, core_ids=list(range(8)))
    outp = np.empty((4, N, D), dtype=np.float32)
    for core in range(8):
        b, hh = core // 2, core % 2
        outp[b, hh * T:(hh + 1) * T] = res.results[core]["out"]
    return outp
